# version 18
# baseline (speedup 1.0000x reference)
from contextlib import ExitStack
import numpy as np
import ml_dtypes
import concourse.bass as bass
import concourse.mybir as mybir
from concourse.bass_utils import run_bass_kernel_spmd

F32 = mybir.dt.float32
BF16 = mybir.dt.bfloat16
AF = mybir.ActivationFunctionType
ALU = mybir.AluOpType
BF = ml_dtypes.bfloat16

D, T, HF = 2048, 2048, 8192
NA, NK = 64, 16
NTB, TB = 4, 512
WSUB = 4
DI, NH, CH, NCH = 4096, 64, 256, 8
FIN = 67
EPS = 1e-5
DBG = False
STOP = 10
SCAN_NCH = NCH
SCAN_NI = 32
SCAN_SUB = 6
RSQ = float(1.0 / np.sqrt(8192.0))


def _hadamard(n):
    h = np.array([[1.0]], dtype=np.float64)
    while h.shape[0] < n:
        h = np.block([[h, h], [h, -h]])
    return h


class Sem:
    __slots__ = ("h", "name", "n")

    def __init__(self, h, name):
        self.h, self.name, self.n = h, name, 0

    def inc(self, k):
        self.n += k
        return self.n


class P:

    ENG_SEM = {"dve": "s_dve", "act": "s_act", "gp": "s_gpc"}

    def __init__(self, cur, eng, sems):
        self.cur = cur
        self.eng = eng
        self.s = sems
        self.wm = {}

    def W(self, eng, sem, val):
        if val is None or val <= 0:
            return
        key = (eng, sem.name)
        if self.wm.get(key, 0) >= val:
            return
        self.wm[key] = val
        if self.cur == eng:
            self.eng.wait_ge(sem.h, val)

    def op(self, eng, build, sem=None, k=1):
        esem_name = self.ENG_SEM.get(eng)
        if esem_name is not None and k != 16:
            esem = self.s[esem_name]
            assert sem is None or sem is esem
            self.W(eng, esem, esem.n)
            sem = esem
            k = 1
        if self.cur == eng:
            ins = build(self.eng)
            if sem is not None:
                ins.then_inc(sem.h, k)
        return sem.inc(k) if sem is not None else None


def build_program():
    nc = bass.Bass("TRN2", target_bir_lowering=False)

    u_d = nc.dram_tensor("u", [D, T], F32, kind="ExternalInput").ap()
    w1_d = nc.dram_tensor("w1a", [NA, 128, NK, 128], F32, kind="ExternalInput").ap()
    w3_d = nc.dram_tensor("w3a", [NA, 128, NK, 128], F32, kind="ExternalInput").ap()
    w2q_d = nc.dram_tensor("w2q", [128, 64 * D], F32, kind="ExternalInput").ap()
    win_d = nc.dram_tensor("wina", [FIN, 128, NK, 128], F32, kind="ExternalInput").ap()
    wout_d = nc.dram_tensor("wouta", [16, 128, 32, 128], F32, kind="ExternalInput").ap()
    convw_d = nc.dram_tensor("convw_d", [128, 136], F32, kind="ExternalInput").ap()
    convb_d = nc.dram_tensor("convb_d", [128, 34], F32, kind="ExternalInput").ap()
    dtb_d = nc.dram_tensor("dtbias_d", [64, 1], F32, kind="ExternalInput").ap()
    alog_d = nc.dram_tensor("alog_d", [64, 1], F32, kind="ExternalInput").ap()
    dpair_d = nc.dram_tensor("dpair_d", [128, 32], F32, kind="ExternalInput").ap()
    nw_d = nc.dram_tensor("normw_d", [128, 32], F32, kind="ExternalInput").ap()
    xo_d = nc.dram_tensor("xo", [D, T], F32, kind="ExternalOutput").ap()

    onesrow_d = nc.inline_tensor(np.ones((1, 128), np.float32), name="onesrowc")
    onescol_d = nc.inline_tensor(np.ones((128, 1), np.float32), name="onescolc")
    idf_d = nc.inline_tensor(np.eye(128, dtype=np.float32), name="idfc")
    idb_d = nc.inline_tensor(np.eye(128, dtype=np.float32).astype(BF), name="idbc")
    k2_d = nc.inline_tensor(np.kron(_hadamard(64), np.eye(2)).astype(BF), name="k2cc")
    h128_d = nc.inline_tensor(_hadamard(128).astype(BF), name="h128cc")
    eps_d = nc.inline_tensor(np.full((1, 1), EPS, np.float32), name="epscc")
    _mk = np.zeros((128, 512), np.float32)
    for _stn in range(2):
        for _p in range(128):
            _mk[_p, _stn * 256 + _stn * 128 + _p:(_stn + 1) * 256] = 1.0
    maskc_d = nc.inline_tensor(_mk, name="maskcc")

    if DBG:
        dbg_pre = nc.dram_tensor("dbg_pre", [3, 64, T], F32).ap()
        dbg_yg = nc.dram_tensor("dbg_yg", [128, 8192], BF16).ap()
        dbg_st = nc.dram_tensor("dbg_st", [128, 4096], F32).ap()
        dbg_bc = nc.dram_tensor("dbg_bc", [2, 128, 256], BF16).ap()
        dbg_m1t = nc.dram_tensor("dbg_m1t", [128, 512], F32).ap()
        dbg_xdt = nc.dram_tensor("dbg_xdt", [2, 128, 4096], BF16).ap()
        dbg_xdd = nc.dram_tensor("dbg_xdd", [2, 128, 4096], BF16).ap()
        dbg_dd = nc.dram_tensor("dbg_dd", [2, 64, 256], F32).ap()
        dbg_tt = nc.dram_tensor("dbg_tt", [3, 128, 128], F32).ap()
    x1_d = nc.dram_tensor("x1s", [D, T], F32).ap()
    x2_d = nc.dram_tensor("x2s", [D, T], F32).ap()
    hid2_d = nc.dram_tensor("hid2s", [NA, NTB, 128, TB], BF16).ap()
    w2mid_d = nc.dram_tensor("w2mids", [NA, 128, D], BF16).ap()
    w2f_d = nc.dram_tensor("w2fs", [NA, 128, D], BF16).ap()
    zs_d = nc.dram_tensor("zss", [32, 128, T], BF16).ap()
    xbc_d = nc.dram_tensor("xbcs", [34, 128, T], BF16).ap()
    yn_d = nc.dram_tensor("yns", [32, 128, T], BF16).ap()
    csr_d = nc.dram_tensor("csrs", [64, T], F32).ap()

    st = ExitStack()
    with st:
        BIG = st.enter_context(nc.sbuf_tensor("BIG", [128, 65536], BF16)).ap()
        S1 = st.enter_context(nc.sbuf_tensor("S1", [128, 12288], BF16)).ap()
        S2 = st.enter_context(nc.sbuf_tensor("S2", [128, 12288], BF16)).ap()

        def fv(ap, c0, nf, p=128):
            out = ap[:, c0:c0 + 2 * nf].bitcast(F32)
            return out if p == 128 else out[0:p, :]

        xln = BIG[:, 0:32768]
        hid2tb = [BIG[:, 32768:65536], BIG[:, 0:32768]]
        w1b = [S1[:, 0:2048], S1[:, 2048:4096]]
        w3b = [S1[:, 4096:6144], S1[:, 6144:8192]]
        hidb = [S2[:, 8192:10240], S2[:, 10240:12288]]
        sgb = [fv(S2, 0, 2048), fv(S2, 4096, 2048)]
        w2fb = [S2[:, 0:4096], S2[:, 4096:8192]]
        xinb = [fv(S2, 8192, 512), fv(S2, 9216, 512)]
        xoutb = [fv(S2, 10240, 512), fv(S2, 11264, 512)]
        winb = [S1[:, 0:2048], S1[:, 2048:4096]]
        m1st = [S1[:, 4096:6144], S1[:, 6144:8192]]
        woutb = [S1[:, 0:4096], S1[:, 4096:8192]]
        lnx = [fv(S1, 0, 2048), fv(S1, 4096, 2048)]
        lnsq = [fv(S1, 8192, 2048), fv(S2, 8192, 2048)]
        mb_sb = fv(S2, 0, 2048)
        rb_sb = fv(S2, 4096, 2048)
        p1in = [S1[:, 0:4096], S1[:, 4096:8192]]
        p1st = [S2[:, 0:512], S2[:, 512:1024], S2[:, 1024:1536], S2[:, 1536:2048]]
        p1m2 = [S1[:, 8192:10240], S1[:, 10240:12288]]
        p1o2 = [S2[:, 2048:4096], S2[:, 4096:6144]]

        xbcc = BIG[:, 0:8806]
        zcq = [BIG[:, 8840:10888], BIG[:, 10888:12936]]
        bct = BIG[:, 13960:14216]
        cct = BIG[:, 14216:14472]
        xdt = [BIG[:, 14472:18568], BIG[:, 18568:22664]]
        xdd = [BIG[:, 22664:26760], BIG[:, 26760:30856]]
        ygb = BIG[:, 30856:39048]
        stf = fv(BIG, 39048, 4096)
        stb = BIG[:, 47240:51336]
        daf = fv(BIG, 51336, 2048, p=64)
        dtf = fv(BIG, 55432, 2048, p=64)
        csf = fv(BIG, 59528, 2048, p=64)
        btok = BIG[:, 63624:63880]
        row0 = fv(BIG, 40000, 2048, p=1)
        row1 = fv(BIG, 44096, 2048, p=1)
        row2 = fv(BIG, 48192, 2048, p=1)
        srms = fv(S1, 0, 512, p=1)

        def sbt(name, shape, dtype):
            return st.enter_context(nc.sbuf_tensor(name, shape, dtype)).ap()

        maskc = sbt("maskc", [128, 512], F32)
        onesrow = sbt("onesrow", [1, 128], F32)
        epsc = sbt("epsc", [1, 1], F32)
        onescol = sbt("onescol", [128, 1], F32)
        idf = sbt("idf", [128, 128], F32)
        idb = sbt("idb", [128, 128], BF16)
        k2c = sbt("k2c", [128, 128], BF16)
        h128c = sbt("h128c", [128, 128], BF16)
        convw = sbt("convw", [128, 136], F32)
        convb = sbt("convb", [128, 34], F32)
        dtbias = sbt("dtbias", [64, 1], F32)
        alogc = sbt("alogc", [64, 1], F32)
        acol = sbt("acol", [64, 1], F32)
        dpair = sbt("dpair", [128, 32], F32)
        normw = sbt("normw", [128, 32], F32)
        zero64 = sbt("zero64", [64, 256], F32)
        csT = sbt("csT", [128, 128], F32)
        dtT = sbt("dtT", [128, 128], F32)
        ddT = sbt("ddT", [128, 128], F32)
        ddb = sbt("ddb", [64, 256], F32)
        decayb = sbt("decayb", [64, 256], F32)
        cdcol = sbt("cdcol", [64, 1], F32)
        cdrow = sbt("cdrow", [1, 64], F32)
        cdecrep = sbt("cdecrep", [128, 64], F32)
        m1t = sbt("m1t", [128, 512], F32)
        decT = sbt("decT", [128, 128], F32)
        a1t4 = [[sbt("a1t0", [128, 512], BF16), sbt("a1t1", [128, 512], BF16)],
                [sbt("a1t2", [128, 512], BF16), sbt("a1t3", [128, 512], BF16)]]
        cet4 = [[sbt("cet0", [128, 256], BF16), sbt("cet1", [128, 256], BF16)],
                [sbt("cet2", [128, 256], BF16), sbt("cet3", [128, 256], BF16)]]
        t1x = sbt("t1x", [128, 256], F32)
        t2x = sbt("t2x", [128, 128], F32)
        cvt = [sbt("cvt0", [128, 256], F32), sbt("cvt1", [128, 256], F32)]
        rbx = sbt("rbx", [128, 256], F32)
        sqx = [sbt("sqx0", [128, 256], F32), sbt("sqx1", [128, 256], F32)]
        xsr = S2[:, 0:4102].bitcast(F32)
        cvf2 = [S2[:, 4352:8448].bitcast(F32), S1[:, 8192:12288].bitcast(F32)]
        xall = S1[:, 1024:9216]
        ebx2 = [fv(S2, 0, 512), fv(S2, 1024, 512)]
        ltm4 = [[fv(S2, 2048, 512), fv(S2, 3072, 512)],
                [fv(S2, 8192, 512), fv(S2, 9216, 512)]]
        ltx4 = [[fv(S2, 4096, 512), fv(S2, 5120, 512)],
                [fv(S2, 10240, 512), fv(S2, 11264, 512)]]
        csrg = [BIG[0:1, k * 2048:(k + 1) * 2048].bitcast(F32)
                for k in range(4)]
        csb2 = [fv(S2, 6144, 512), fv(S2, 7168, 512)]

        PS = st.enter_context(nc.psum_tensor("PSf", [128, 4096], F32)).ap()
        PSB = PS[:, 3584:4096].bitcast(BF16)

        def bank(i, w=512):
            return PS[:, i * 512:i * 512 + w]

        sem_handles = {}
        for name in ("s_c", "s_pe", "s_act", "s_dve", "s_gpc",
                     "p1i0", "p1i1", "p1w0", "p1w1", "p1w2", "p1w3",
                     "p1r0", "p1r1", "p1o0", "p1o1",
                     "lnx0", "lnx1", "w130", "w131", "hdw0", "hdw1", "hdr",
                     "w2f0", "w2f1", "xin0", "xin1", "xou0", "xou1",
                     "win0", "win1", "m1w0", "m1w1",
                     "xbc", "z0", "z1", "ynw", "ynr", "wo0", "wo1",
                     "cso", "csi0", "csi1", "csi2", "csi3",
                     "pf1", "pf3", "hdb", "xga", "xgb"):
            sem_handles[name] = st.enter_context(nc.semaphore(name))

        def emit(p: P):
            S = p.s
            s_c, s_pe, s_act, s_dve, s_gpc = (
                S[n] for n in ("s_c", "s_pe", "s_act", "s_dve", "s_gpc"))
            s_p1i = [S["p1i0"], S["p1i1"]]
            s_p1w = [S["p1w0"], S["p1w1"], S["p1w2"], S["p1w3"]]
            s_p1r = [S["p1r0"], S["p1r1"]]
            s_p1o = [S["p1o0"], S["p1o1"]]
            s_lnx = [S["lnx0"], S["lnx1"]]
            s_w13 = [S["w130"], S["w131"]]
            s_hdw = [S["hdw0"], S["hdw1"]]
            s_hdr2 = [S["hdr"], S["hdb"]]
            s_w2f = [S["w2f0"], S["w2f1"]]
            s_xin = [S["xin0"], S["xin1"]]
            s_xou = [S["xou0"], S["xou1"]]
            s_win = [S["win0"], S["win1"]]
            s_m1w = [S["m1w0"], S["m1w1"]]
            s_xbc = S["xbc"]
            s_cso = S["cso"]
            s_pfs = {1: S["pf1"], 3: S["pf3"]}
            s_xg = [S["xga"], S["xgb"]]
            s_csi = [S["csi0"], S["csi1"], S["csi2"], S["csi3"]]
            s_z = [S["z0"], S["z1"]]
            s_ynw = S["ynw"]
            s_ynr = S["ynr"]
            s_wo = [S["wo0"], S["wo1"]]

            for dst, src in ((onesrow, onesrow_d.ap()), (onescol, onescol_d.ap()),
                             (idf, idf_d.ap()), (idb, idb_d.ap()),
                             (k2c, k2_d.ap()), (h128c, h128_d.ap()),
                             (epsc, eps_d.ap()), (maskc, maskc_d.ap()),
                             (convw, convw_d), (convb, convb_d),
                             (dtbias, dtb_d), (alogc, alog_d), (dpair, dpair_d),
                             (normw, nw_d)):
                p.op("sync", lambda e, d=dst, s=src: e.dma_start(d, s), s_c, 16)
            nconst = s_c.n
            p.op("gp", lambda e: e.memset(zero64, 0.0), s_gpc, 1)
            gp_init = s_gpc.n

            p.W("gp", s_c, nconst)
            pe_ch, dv_sl = [], []
            slot_w = [0, 0, 0, 0]
            for j in range(32):
                if j >= 2:
                    p.W("gp", s_pe, pe_ch[j - 2])
                dn = p.op("gp", lambda e, j=j: e.dma_start(
                    p1in[j % 2], w2q_d[:, j * 4096:(j + 1) * 4096]),
                    s_p1i[j % 2], 16)
                p.W("pe", s_p1i[j % 2], dn)
                p.W("pe", s_c, nconst)
                if j >= 1:
                    p.W("pe", s_dve, dv_sl[-1])
                base_pe = s_pe.n
                for m in range(8):
                    p.op("pe", lambda e, j=j, m=m: e.matmul(
                        bank(m), k2c[:], p1in[j % 2][:, m * 512:(m + 1) * 512],
                        start=True, stop=True), s_pe, 1)
                pe_ch.append(s_pe.n)
                for m in range(8):
                    sl = j * 8 + m
                    p.W("dve", s_pe, base_pe + m + 1)
                    if sl >= 4:
                        p.W("dve", s_p1w[sl % 4], slot_w[sl % 4])
                    dv_sl.append(p.op("dve", lambda e, j=j, m=m, sl=sl:
                                      e.tensor_scalar(
                        p1st[sl % 4], bank(m), RSQ, None, ALU.mult), s_dve, 1))
                    p.W("sync", s_dve, dv_sl[sl])
                    bcol, half = 4 * j + m // 2, m % 2
                    dst = bass.AP(w2mid_d.tensor, bcol * 2048 + half * 512,
                                  [[262144, 64], [1024, 2], [1, 512]])
                    slot_w[sl % 4] = p.op("sync", lambda e, sl=sl, dst=dst:
                                          e.dma_start(
                        dst, p1st[sl % 4]), s_p1w[sl % 4], 16)
            p1w_tot = [s.n for s in s_p1w]
            in2, pe2, dv2, o2 = [], [], [], []
            for a in range(NA):
                for t in range(4):
                    p.W("sync", s_p1w[t], p1w_tot[t])
                if a >= 2:
                    p.W("sync", s_pe, pe2[a - 2])
                in2.append(p.op("sync", lambda e, a=a: e.dma_start(
                    p1m2[a % 2], w2mid_d[a]), s_p1r[a % 2], 16))
                p.W("pe", s_p1r[a % 2], in2[a])
                if a >= 2:
                    p.W("pe", s_dve, dv2[a - 2])
                for m in range(4):
                    p.op("pe", lambda e, a=a, m=m: e.matmul(
                        bank((a % 2) * 4 + m), h128c[:],
                        p1m2[a % 2][:, m * 512:(m + 1) * 512],
                        start=True, stop=True), s_pe, 1)
                pe2.append(s_pe.n)
                p.W("dve", s_pe, pe2[a])
                if a >= 2:
                    p.W("dve", s_p1o[a % 2], o2[a - 2])
                for m in range(4):
                    p.op("dve", lambda e, a=a, m=m: e.tensor_copy(
                        p1o2[a % 2][:, m * 512:(m + 1) * 512],
                        bank((a % 2) * 4 + m)), s_dve, 1)
                dv2.append(s_dve.n)
                p.W("sync", s_dve, dv2[a])
                o2.append(p.op("sync", lambda e, a=a: e.dma_start(
                    w2f_d[a], p1o2[a % 2]), s_p1o[a % 2], 16))
            w2f_ready = [(s_p1o[0], s_p1o[0].n), (s_p1o[1], s_p1o[1].n),
                         (s_p1w[0], p1w_tot[0]), (s_p1w[1], p1w_tot[1]),
                         (s_p1w[2], p1w_tot[2]), (s_p1w[3], p1w_tot[3]),
                         (s_p1r[0], s_p1r[0].n), (s_p1r[1], s_p1r[1].n)]
            p1_pe_end = pe2[-1]
            p1_dve_end = dv2[-1]

            def emit_ln(src_d, waits):
                for eng in ("act", "pe", "dve", "gp"):
                    for sm, v in waits:
                        p.W(eng, sm, v)
                ld1, sqd, per = [], [], []
                for k in range(NK):
                    if k >= 2:
                        p.W("act", s_pe, per[k - 2])
                    ld1.append(p.op("act", lambda e, k=k: e.dma_start(
                        lnx[k % 2], src_d[k * 128:(k + 1) * 128, :]),
                        s_lnx[k % 2], 16))
                    p.W("act", s_lnx[k % 2], ld1[k])
                    sqd.append(p.op("act", lambda e, k=k: e.activation(
                        lnsq[k % 2], lnx[k % 2], AF.Square), s_act, 1))
                    p.W("pe", s_lnx[k % 2], ld1[k])
                    p.W("pe", s_act, sqd[k])
                    for m in range(4):
                        p.op("pe", lambda e, k=k, m=m: e.matmul(
                            PS[0:1, m * 512:(m + 1) * 512], onescol[:],
                            lnx[k % 2][:, m * 512:(m + 1) * 512],
                            start=(k == 0), stop=(k == NK - 1)), s_pe, 1)
                    for m in range(4):
                        p.op("pe", lambda e, k=k, m=m: e.matmul(
                            PS[0:1, 2048 + m * 512:2048 + (m + 1) * 512],
                            onescol[:],
                            lnsq[k % 2][:, m * 512:(m + 1) * 512],
                            start=(k == 0), stop=(k == NK - 1)), s_pe, 1)
                    per.append(s_pe.n)
                p.W("dve", s_pe, per[-1])
                p.op("dve", lambda e: e.tensor_scalar(row0, PS[0:1, 0:2048],
                                                      1.0 / D, None, ALU.mult),
                     s_dve, 1)
                p.op("dve", lambda e: e.tensor_scalar(row1, PS[0:1, 2048:4096],
                                                      1.0 / D, None, ALU.mult),
                     s_dve, 1)
                p.op("dve", lambda e: e.tensor_tensor(row2, row0, row0, ALU.mult),
                     s_dve, 1)
                nv = p.op("dve", lambda e: e.tensor_sub(row1, row1, row2), s_dve, 1)
                p.W("act", s_dve, nv)
                p.op("act", lambda e: e.activation(row1, row1, AF.Ln, bias=epsc),
                     s_act, 1)
                nr = p.op("act", lambda e: e.activation(row1, row1, AF.Exp,
                                                        scale=-0.5), s_act, 1)
                p.W("pe", s_act, nr)
                for m in range(4):
                    p.op("pe", lambda e, m=m: e.matmul(
                        PS[:, m * 512:(m + 1) * 512], onesrow[:],
                        row0[:, m * 512:(m + 1) * 512], start=True, stop=True),
                        s_pe, 1)
                for m in range(4):
                    p.op("pe", lambda e, m=m: e.matmul(
                        PS[:, 2048 + m * 512:2048 + (m + 1) * 512], onesrow[:],
                        row1[:, m * 512:(m + 1) * 512], start=True, stop=True),
                        s_pe, 1)
                nbc2 = s_pe.n
                p.W("dve", s_pe, nbc2)
                p.op("dve", lambda e: e.tensor_copy(mb_sb, PS[:, 0:2048]), s_dve, 1)
                mbrb = p.op("dve", lambda e: e.tensor_copy(rb_sb, PS[:, 2048:4096]),
                            s_dve, 1)
                ld2, gpd, apd = [], [], []
                for k in range(NK):
                    if k >= 2:
                        p.W("act", s_gpc, gpd[k - 2])
                    ld2.append(p.op("act", lambda e, k=k: e.dma_start(
                        lnx[k % 2], src_d[k * 128:(k + 1) * 128, :]),
                        s_lnx[k % 2], 16))
                    p.W("gp", s_lnx[k % 2], ld2[k])
                    p.W("gp", s_dve, mbrb)
                    if k >= 2:
                        p.W("gp", s_dve, apd[k - 2])
                    gpd.append(p.op("gp", lambda e, k=k: e.tensor_sub(
                        lnsq[k % 2], lnx[k % 2], mb_sb), s_gpc, 1))
                    p.W("dve", s_gpc, gpd[k])
                    apd.append(p.op("dve", lambda e, k=k: e.tensor_mul(
                        xln[:, k * T:(k + 1) * T], lnsq[k % 2], rb_sb), s_dve, 1))
                return apd[-1], s_gpc.n

            def emit_ffn(src_d, dst_d, ln_done, pf_sem, pf_gates):
                ln_dve, ln_gpc = ln_done
                for sm, v in pf_gates:
                    p.W("sync", sm, v)
                npf = p.op("sync", lambda e: e.dma_start(
                    dst_d, src_d), pf_sem, 16)
                wdma, peg, pel, sil, mul, hw = [], [], [], [], [], []
                for a in range(NA):
                    if a >= 2:
                        p.W("gp", s_pe, pel[a - 2])
                    else:
                        p.W("gp", s_gpc, ln_gpc)
                    p.op("gp", lambda e, a=a: e.dma_start(
                        w1b[a % 2], w1_d[a].rearrange("p k h -> p (k h)")),
                        s_w13[a % 2], 16)
                    wdma.append(p.op("gp", lambda e, a=a: e.dma_start(
                        w3b[a % 2], w3_d[a].rearrange("p k h -> p (k h)")),
                        s_w13[a % 2], 16))
                    p.W("pe", s_w13[a % 2], wdma[a])
                    p.W("pe", s_dve, ln_dve)
                    if a >= 1:
                        p.W("pe", s_act, sil[a - 1])
                    for k in range(NK):
                        for tb in range(NTB):
                            p.op("pe", lambda e, a=a, k=k, tb=tb: e.matmul(
                                bank(tb), w1b[a % 2][:, k * 128:(k + 1) * 128],
                                xln[:, k * T + tb * TB:k * T + (tb + 1) * TB],
                                start=(k == 0), stop=(k == NK - 1)),
                                s_pe if (k == NK - 1 and tb == NTB - 1) else None)
                    peg.append(s_pe.n)
                    if a >= 1:
                        p.W("pe", s_dve, mul[a - 1])
                    for k in range(NK):
                        for tb in range(NTB):
                            p.op("pe", lambda e, a=a, k=k, tb=tb: e.matmul(
                                bank(4 + tb),
                                w3b[a % 2][:, k * 128:(k + 1) * 128],
                                xln[:, k * T + tb * TB:k * T + (tb + 1) * TB],
                                start=(k == 0), stop=(k == NK - 1)),
                                s_pe if (k == NK - 1 and tb == NTB - 1) else None)
                    pel.append(s_pe.n)
                    p.W("act", s_pe, peg[a])
                    if a >= 1:
                        p.W("act", s_dve, mul[a - 1])
                    sil.append(p.op("act", lambda e, a=a: e.activation(
                        sgb[a % 2], PS[:, 0:2048], AF.Silu), s_act, 1))
                    p.W("dve", s_pe, pel[a])
                    p.W("dve", s_act, sil[a])
                    if a >= 2:
                        p.W("dve", s_hdw[a % 2], hw[a - 2])
                    mul.append(p.op("dve", lambda e, a=a: e.tensor_tensor(
                        hidb[a % 2], sgb[a % 2], PS[:, 2048:4096], ALU.mult),
                        s_dve, 1))
                    p.W("sync", s_dve, mul[a])
                    hw.append(p.op("sync", lambda e, a=a: e.dma_start(
                        hid2_d[a].rearrange("tb p t -> p tb t"),
                        hidb[a % 2].rearrange("p (tb t) -> p tb t", tb=NTB)),
                        s_hdw[a % 2], 16))
                hid_all = [(s_hdw[0], s_hdw[0].n), (s_hdw[1], s_hdw[1].n)]
                htb, wfd, pea, evj, xoj, xij = [], [], [], [], [], []

                def w2f_batch(wb):
                    bi_w, sub_w = divmod(wb, NA // WSUB)
                    half_w = bi_w % 2
                    a0 = sub_w * WSUB
                    if wb >= 2:
                        p.W("sync", s_pe, pea[-1])
                    else:
                        p.W("sync", s_dve, mul[-1])
                    for sm, v in w2f_ready:
                        p.W("sync", sm, v)
                    wfd.append(p.op("sync", lambda e, a0=a0, half_w=half_w, wb=wb:
                                    e.dma_start(
                        w2fb[wb % 2].rearrange("p (a d) -> p a d", a=WSUB),
                        w2f_d[a0:a0 + WSUB, :,
                              half_w * 1024:(half_w + 1) * 1024]
                        .rearrange("a p d -> p a d")), s_w2f[wb % 2], 16))

                w2f_batch(0)
                w2f_batch(1)
                nwb = 2
                for tb in range(NTB):
                    for sm, v in hid_all:
                        p.W("act", sm, v)
                    if tb >= 2:
                        p.W("act", s_pe, pea[(tb - 1) * 2 * NA - 1])
                    for grp in range(8):
                        p.op("act", lambda e, tb=tb, grp=grp: e.dma_start(
                            hid2tb[tb % 2][:, grp * 8 * TB:(grp + 1) * 8 * TB]
                            .rearrange("p (a t) -> p a t", a=8),
                            hid2_d[grp * 8:(grp + 1) * 8, tb]
                            .rearrange("a p t -> p a t")), s_hdr2[tb % 2], 16)
                    htb.append(s_hdr2[tb % 2].n)
                    for half in range(2):
                        bi = tb * 2 + half
                        p.W("pe", s_hdr2[tb % 2], htb[tb])
                        if bi > 0:
                            p.W("pe", s_dve, evj[bi * 8 - 1])
                        for a in range(NA):
                            if a % WSUB == 0 and not (bi == 0 and a == 0):
                                if nwb < 2 * NA:
                                    w2f_batch(nwb)
                                    nwb += 1
                            wb = bi * (NA // WSUB) + a // WSUB
                            p.W("pe", s_w2f[wb % 2], wfd[wb])
                            for dmt in range(8):
                                p.op("pe", lambda e, a=a, dmt=dmt, tb=tb, wb=wb:
                                     e.matmul(
                                    bank(dmt),
                                    w2fb[wb % 2][:, (a % WSUB) * 1024 + dmt * 128:
                                                 (a % WSUB) * 1024 + dmt * 128 + 128],
                                    hid2tb[tb % 2][:, a * TB:(a + 1) * TB],
                                    start=(a == 0), stop=(a == NA - 1)),
                                    s_pe if dmt == 7 else None)
                            pea.append(s_pe.n)
                        for dmt in range(8):
                            j = bi * 8 + dmt
                            dm0 = half * 8 + dmt
                            p.W("dve", s_pe, pea[-1])
                            if j < 2:
                                for sm, v in hid_all:
                                    p.W("dve", sm, v)
                            if j >= 2:
                                p.W("dve", s_xg[j % 2], xoj[j - 2])
                            evj.append(p.op("dve", lambda e, dmt=dmt, j=j:
                                            e.tensor_scalar(
                                xoutb[j % 2], bank(dmt), 0.5, None,
                                ALU.mult), s_dve, 1))
                            p.W("gp", s_dve, evj[j])
                            p.W("gp", pf_sem, npf)
                            xoj.append(p.op("gp", lambda e, dm0=dm0, tb=tb, j=j:
                                            e.dma_start(
                                dst_d[dm0 * 128:(dm0 + 1) * 128,
                                      tb * TB:(tb + 1) * TB], xoutb[j % 2],
                                accum_op=ALU.add), s_xg[j % 2], 16))
                return (pea[-1], evj[-1],
                        [(s_xg[0], s_xg[0].n), (s_xg[1], s_xg[1].n)])

            def emit_m1(ln_done):
                ln_dve, ln_gpc = ln_done
                wdma, pef, dvf, syf = [], [], [], []
                nzp = p.op("dve", lambda e: e.memset(xsr[:, 0:3], 0.0), s_dve, 1)
                nslp = {}
                for f in range(FIN):
                    if f >= 2:
                        p.W("gp", s_pe, pef[f - 2])
                    else:
                        p.W("gp", s_gpc, ln_gpc)
                    wdma.append(p.op("gp", lambda e, f=f: e.dma_start(
                        winb[f % 2], win_d[f].rearrange("p k h -> p (k h)")),
                        s_win[f % 2], 16))
                    p.W("pe", s_win[f % 2], wdma[f])
                    p.W("pe", s_dve, ln_dve)
                    if f >= 2:
                        p.W("pe", s_dve, dvf[f - 2])
                    for k in range(NK):
                        for tb in range(NTB):
                            p.op("pe", lambda e, f=f, k=k, tb=tb: e.matmul(
                                bank((f % 2) * 4 + tb),
                                winb[f % 2][:, k * 128:(k + 1) * 128],
                                xln[:, k * T + tb * TB:k * T + (tb + 1) * TB],
                                start=(k == 0), stop=(k == NK - 1)),
                                s_pe if (k == NK - 1 and tb == NTB - 1) else None)
                    pef.append(s_pe.n)
                    p.W("dve", s_pe, pef[f])
                    if f == 66:
                        p.op("dve", lambda e, f=f: e.tensor_copy(
                            daf, PS[0:64, (f % 2) * 2048:(f % 2) * 2048 + 2048]),
                            s_dve, 1)
                        dvf.append(s_dve.n)
                        syf.append(None)
                    elif f < 32:
                        if f >= 2 and syf[f - 2] is not None:
                            p.W("dve", s_m1w[f % 2], syf[f - 2])
                        p.op("dve", lambda e, f=f: e.tensor_copy(
                            m1st[f % 2],
                            PS[:, (f % 2) * 2048:(f % 2) * 2048 + 2048]), s_dve, 1)
                        dvf.append(s_dve.n)
                        p.W("sync", s_dve, dvf[f])
                        syf.append(p.op("sync", lambda e, f=f: e.dma_start(
                            zs_d[f], m1st[f % 2]), s_m1w[f % 2], 16))
                    else:
                        tile = f - 32
                        cvf = cvf2[f % 2]
                        p.op("dve", lambda e, f=f: e.tensor_copy(
                            xsr[:, 3:2051],
                            PS[:, (f % 2) * 2048:(f % 2) * 2048 + 2048]), s_dve, 1)
                        p.W("dve", s_act, nslp.get(f % 2, 0))
                        p.op("dve", lambda e, tile=tile, cvf=cvf: e.tensor_scalar(
                            cvf, xsr[:, 0:2048],
                            convw[:, tile * 4:tile * 4 + 1], None, ALU.mult),
                            s_dve, 1)
                        for kk in range(1, 4):
                            p.op("dve", lambda e, tile=tile, kk=kk, cvf=cvf:
                                 e.scalar_tensor_tensor(
                                cvf, xsr[:, kk:kk + 2048],
                                convw[:, tile * 4 + kk:tile * 4 + kk + 1], cvf,
                                ALU.mult, ALU.add), s_dve, 1)
                        dvf.append(s_dve.n)
                        p.W("act", s_dve, dvf[f])
                        if f >= 2 and syf[f - 2] is not None:
                            p.W("act", s_m1w[f % 2], syf[f - 2])
                        nslp[f % 2] = p.op("act", lambda e, f=f, tile=tile,
                                           cvf=cvf: e.activation(
                            m1st[f % 2], cvf, AF.Silu,
                            bias=convb[:, tile:tile + 1]), s_act, 1)
                        p.W("sync", s_act, nslp[f % 2])
                        syf.append(p.op("sync", lambda e, tile=tile, f=f:
                                        e.dma_start(
                            xbc_d[tile], m1st[f % 2]), s_m1w[f % 2], 16))
                return pef[-1], dvf[-1], [(s_m1w[0], s_m1w[0].n),
                                          (s_m1w[1], s_m1w[1].n)]

            def emit_prescan(m1_dve):
                p.W("gp", s_dve, m1_dve)
                p.op("gp", lambda e: e.memset(stf, 0.0), s_gpc, 1)
                st_init = p.op("gp", lambda e: e.memset(stb, 0.0), s_gpc, 1)
                p.W("act", s_c, nconst)
                na = p.op("act", lambda e: e.activation(acol, alogc, AF.Exp),
                          s_act, 1)
                p.W("dve", s_act, na)
                p.W("dve", s_dve, m1_dve)
                p.op("dve", lambda e: e.tensor_scalar(acol, acol, -1.0, None,
                                                      ALU.mult), s_dve, 1)
                p.op("dve", lambda e: e.tensor_scalar(dtf, daf, dtbias, None,
                                                      ALU.add), s_dve, 1)
                p.op("dve", lambda e: e.tensor_scalar(csf, dtf, -1.0, None,
                                                      ALU.mult), s_dve, 1)
                nv = p.op("dve", lambda e: e.tensor_tensor(csf, dtf, csf,
                                                           ALU.max), s_dve, 1)
                p.W("act", s_dve, nv)
                p.op("act", lambda e: e.activation(csf, csf, AF.Exp, scale=-1.0),
                     s_act, 1)
                na = p.op("act", lambda e: e.activation(
                    csf, csf, AF.Ln, bias=onescol[0:64, :]), s_act, 1)
                p.W("dve", s_act, na)
                p.op("dve", lambda e: e.tensor_scalar(dtf, dtf, 0.0, None, ALU.max),
                     s_dve, 1)
                p.op("dve", lambda e: e.tensor_tensor(dtf, dtf, csf, ALU.add),
                     s_dve, 1)
                p.op("dve", lambda e: e.tensor_scalar(daf, dtf, acol, None,
                                                      ALU.mult), s_dve, 1)
                p.W("dve", s_gpc, gp_init)
                nv = None
                for c in range(NCH):
                    nv = p.op("dve", lambda e, c=c: e.tensor_tensor_scan(
                        csf[:, c * CH:(c + 1) * CH], daf[:, c * CH:(c + 1) * CH],
                        zero64, 0.0, ALU.add, ALU.add), s_dve, 1)
                if DBG:
                    p.W("sync", s_dve, nv)
                    for di, src in enumerate((daf, dtf, csf)):
                        p.op("sync", lambda e, di=di, src=src: e.dma_start(
                            dbg_pre[di], src), s_ynr, 16)
                return nv, st_init

            def emit_scan(prescan_dve, st_init, m1_pe, m1_sy):
                b6 = bank(6)
                cmax = [(s_pe, m1_pe), (s_dve, prescan_dve),
                        (s_gpc, st_init)] + list(m1_sy)
                updd, stbd = {}, {}
                prev_t1d_end = 0
                p.W("sync", s_dve, prescan_dve)
                ncso = p.op("sync", lambda e: e.dma_start(csr_d, csf), s_cso, 16)
                for c in range(SCAN_NCH):
                    for eng in ("pe", "act", "dve", "gp", "sync"):
                        for sm, v in cmax:
                            p.W(eng, sm, v)
                    for t0, t1 in ((0, 8), (8, 16), (16, 24), (24, 32)):
                        p.op("sync", lambda e, c=c, t0=t0, t1=t1: e.dma_start(
                            xall[:, t0 * 256:t1 * 256]
                            .rearrange("p (a t) -> p a t", a=t1 - t0),
                            xbc_d[t0:t1, :, c * 256:(c + 1) * 256]
                            .rearrange("a p t -> p a t")), s_xbc, 16)
                    p.op("sync", lambda e, c=c: e.dma_start(
                        bct, xbc_d[32, :, c * 256:(c + 1) * 256]), s_xbc, 16)
                    dxb = p.op("sync", lambda e, c=c: e.dma_start(
                        cct, xbc_d[33, :, c * 256:(c + 1) * 256]), s_xbc, 16)
                    dzq = [p.op("sync", lambda e, c=c: e.dma_start(
                        zcq[0].rearrange("p (a t) -> p a t", a=8),
                        zs_d[0:8, :, c * 256:(c + 1) * 256]
                        .rearrange("a p t -> p a t")), s_z[0], 16)]
                    cend = c * CH + CH - 1
                    dec = p.op("act", lambda e, c=c, cend=cend: e.activation(
                        decayb, csf[:, c * CH:(c + 1) * CH], AF.Exp, scale=-1.0,
                        bias=csf[:, cend:cend + 1]), s_act, 1)
                    cdn = p.op("act", lambda e, cend=cend: e.activation(
                        cdcol, csf[:, cend:cend + 1], AF.Exp), s_act, 1)
                    p.W("dve", s_act, dec)
                    ddn = p.op("dve", lambda e, c=c: e.tensor_tensor(
                        ddb, dtf[:, c * CH:(c + 1) * CH], decayb, ALU.mult),
                        s_dve, 1)
                    prevev = 0
                    for (srcm, dstm, gate) in ((csf, csT, None), (dtf, dtT, None),
                                               (None, ddT, ddn)):
                        for hh in range(2):
                            if gate:
                                p.W("pe", s_dve, gate)
                            p.W("pe", s_act, prevev)
                            if srcm is not None:
                                n = p.op("pe", lambda e, c=c, hh=hh, srcm=srcm:
                                         e.transpose(
                                    b6[:, hh * 128:hh * 128 + 64],
                                    srcm[:, c * CH + hh * 128:
                                         c * CH + (hh + 1) * 128],
                                    idf[0:64, 0:64]), s_pe, 1)
                            else:
                                n = p.op("pe", lambda e, hh=hh: e.transpose(
                                    b6[:, hh * 128:hh * 128 + 64],
                                    ddb[:, hh * 128:(hh + 1) * 128],
                                    idf[0:64, 0:64]), s_pe, 1)
                            p.W("act", s_pe, n)
                            prevev = p.op("act", lambda e, hh=hh, dstm=dstm: e.copy(
                                dstm[:, hh * 64:(hh + 1) * 64],
                                b6[:, hh * 128:hh * 128 + 64]), s_act, 1)
                    p.W("pe", s_act, cdn)
                    p.W("pe", s_act, prevev)
                    n = p.op("pe", lambda e: e.transpose(b6[0:1, 256:320], cdcol,
                                                         idf[0:64, 0:64]), s_pe, 1)
                    p.W("act", s_pe, n)
                    ncd = p.op("act", lambda e: e.copy(cdrow, b6[0:1, 256:320]),
                               s_act, 1)
                    p.W("pe", s_act, ncd)
                    n = p.op("pe", lambda e: e.matmul(b6[:, 384:448], onesrow[:],
                                                      cdrow, start=True,
                                                      stop=True), s_pe, 1)
                    p.W("dve", s_pe, n)
                    ncdr = p.op("dve", lambda e: e.tensor_copy(
                        cdecrep, b6[:, 384:448]), s_dve, 1)

                    p.W("pe", s_act, dec)
                    p.W("pe", s_dve, ncdr)
                    for hh in range(2):
                        p.W("pe", s_act, prevev)
                        n = p.op("pe", lambda e, hh=hh: e.transpose(
                            b6[:, hh * 128:hh * 128 + 64],
                            decayb[:, hh * 128:(hh + 1) * 128],
                            idf[0:64, 0:64]), s_pe, 1)
                        p.W("act", s_pe, n)
                        prevev = p.op("act", lambda e, hh=hh: e.copy(
                            decT[:, hh * 64:(hh + 1) * 64],
                            b6[:, hh * 128:hh * 128 + 64]), s_act, 1)

                    p.W("pe", s_xbc, dxb)
                    for stn in range(2):
                        p.op("pe", lambda e, stn=stn: e.transpose(
                            PSB[:, stn * 128:(stn + 1) * 128],
                            bct[:, stn * 128:(stn + 1) * 128], idb), s_pe, 1)
                    ntb2 = s_pe.n
                    ntl = []
                    for stn in range(2):
                        p.W("act", s_pe, ntb2)
                        ntl.append(p.op("act", lambda e, stn=stn: e.copy(
                            btok[:, stn * 128:(stn + 1) * 128],
                            PSB[:, stn * 128:(stn + 1) * 128]), s_act, 1))
                    nm1 = []
                    for stn in range(2):
                        if stn == 1:
                            p.W("pe", s_dve, nm1[0])
                        n = p.op("pe", lambda e, stn=stn: e.matmul(
                            bank(5)[:, stn * 256:(stn + 1) * 256],
                            bct[:, stn * 128:(stn + 1) * 128], cct,
                            start=True, stop=True), s_pe, 1)
                        p.W("dve", s_pe, n)
                        p.W("dve", s_c, nconst)
                        nm1.append(p.op("dve", lambda e, stn=stn: e.tensor_tensor(
                            m1t[:, stn * 256:(stn + 1) * 256],
                            bank(5)[:, stn * 256:(stn + 1) * 256],
                            maskc[:, stn * 256:(stn + 1) * 256],
                            ALU.mult), s_dve, 1))
                    p.W("act", s_z[0], dzq[0])
                    if c > 0:
                        p.W("act", s_gpc, prev_t1d_end)
                    nzs = {0: p.op("act", lambda e: e.activation(
                        zcq[0], zcq[0], AF.Silu), s_act, 1)}

                    d1eb, d1lt, d1le, d1mm = {}, {}, {}, {}
                    nga, ncet = {}, {}
                    xdone, xtr, t1d, pe_y, xddn = {}, {}, {}, {}, {}
                    csid = {}

                    d1ev, s4d, nt1x = {}, {}, {}

                    def emit_post(j):
                        ybj = bank(2 + (j % 2), 256)
                        qj = j // 8
                        if SCAN_SUB >= 5:
                            p.W("gp", s_pe, s4d[j])
                            p.W("gp", s_dve, ncdr)
                            p.W("gp", s_gpc, nga.get((j, 1), 0))
                            p.op("gp", lambda e, j=j: e.tensor_tensor(
                                t2x.rearrange("p (a b) -> p a b", a=2),
                                stf[:, j * 128:(j + 1) * 128]
                                .rearrange("p (a b) -> p a b", a=2),
                                cdecrep[:, 2 * j:2 * j + 2].unsqueeze(2)
                                .broadcast_to([128, 2, 64]), ALU.mult), s_gpc, 1)
                            p.W("dve", s_gpc, s_gpc.n)
                            nup = p.op("dve", lambda e, j=j: e.tensor_tensor(
                                stf[:, j * 128:(j + 1) * 128], t2x,
                                bank(4)[:, (j % 2) * 128:(j % 2) * 128 + 128],
                                ALU.add), s_dve, 1)
                            updd[(c, j)] = nup
                            p.W("gp", s_dve, nup)
                            stbd[(c, j)] = p.op("gp", lambda e, j=j:
                                                e.tensor_copy(
                                stb[:, j * 128:(j + 1) * 128],
                                stf[:, j * 128:(j + 1) * 128]), s_gpc, 1)
                        if SCAN_SUB >= 6:
                            p.W("dve", s_pe, pe_y[j])
                            p.W("dve", s_c, nconst)
                            p.W("dve", s_xbc, dxb)
                            nt1x[j] = p.op("dve", lambda e, j=j, ybj=ybj:
                                           e.scalar_tensor_tensor(
                                t1x, xall[:, j * 256:(j + 1) * 256],
                                dpair[:, j:j + 1], ybj,
                                ALU.mult, ALU.add), s_dve, 1)
                            p.W("gp", s_dve, nt1x[j])
                            p.W("gp", s_act, nzs[qj])
                            t1d[j] = p.op("gp", lambda e, j=j, qj=qj:
                                          e.tensor_tensor(
                                ygb[:, j * 256:(j + 1) * 256], t1x,
                                zcq[qj % 2][:, (j % 8) * 256:(j % 8 + 1) * 256],
                                ALU.mult), s_gpc, 1)
                        else:
                            nt1x[j] = s_dve.n

                    def emit_xt(k):
                        p.W("pe", s_xbc, dxb)
                        p.W("pe", s_act, ntl[-1])
                        p.W("pe", s_act, xdone.get(k - 1, 0))
                        p.W("pe", s_dve, xddn.get(k - 1, 0))
                        for stn in range(2):
                            p.op("pe", lambda e, k=k, stn=stn: e.transpose(
                                PSB[:, 512 + ((k % 2) * 2 + stn) * 128:
                                    512 + ((k % 2) * 2 + stn) * 128 + 128],
                                xall[:, k * 256 + stn * 128:
                                     k * 256 + (stn + 1) * 128], idb),
                                s_pe, 1)
                        xtr[k] = s_pe.n
                        for stn in range(2):
                            p.W("act", s_pe, xtr[k])
                            for half in range(2):
                                base = 512 + ((k % 2) * 2 + stn) * 128 + half * 64
                                p.op("act", lambda e, k=k, stn=stn, half=half,
                                     base=base: e.activation(
                                    xdt[stn][:, k * 128 + half * 64:
                                             k * 128 + half * 64 + 64],
                                    PSB[:, base:base + 64], AF.Copy,
                                    scale=dtT[:, stn * 64 + 2 * k + half:
                                              stn * 64 + 2 * k + half + 1]),
                                    s_act, 1)
                        xdone[k] = s_act.n
                        for stn in range(2):
                            p.W("dve", s_act, xdone[k])
                            for half in range(2):
                                p.op("dve", lambda e, k=k, stn=stn, half=half:
                                     e.tensor_scalar(
                                    xdd[stn][:, k * 128 + half * 64:
                                             k * 128 + half * 64 + 64],
                                    xdt[stn][:, k * 128 + half * 64:
                                             k * 128 + half * 64 + 64],
                                    decT[:, stn * 64 + 2 * k + half:
                                         stn * 64 + 2 * k + half + 1],
                                    None, ALU.mult), s_dve, 1)
                        xddn[k] = s_dve.n

                    def csr_in(j, gate_pe):
                        p.W("sync", s_pe, gate_pe)
                        p.W("sync", s_cso, ncso)
                        csid[j] = p.op("sync", lambda e, j=j, c=c: e.dma_start(
                            csrg[j % 4].rearrange("o (h t) -> o h t", h=4),
                            csr_d[4 * j:4 * j + 4, c * CH:(c + 1) * CH]
                            .unsqueeze(0)), s_csi[j % 4], 16)

                    def emit_bcast(k):
                        p.W("pe", s_csi[(k // 2) % 4], csid[k // 2])
                        p.W("pe", s_dve, d1ev.get(k - 2, 0))
                        d1mm[k] = p.op("pe", lambda e, k=k: e.matmul(
                            bank(k % 2), onesrow[:],
                            csrg[(k // 2) % 4][0:1,
                                               (k % 2) * 512:(k % 2) * 512 + 512],
                            start=True, stop=True), s_pe, 1)
                        if k % 2 == 1 and k // 2 + 4 < 16:
                            csr_in(k // 2 + 4, d1mm[k])

                    for j in range(4):
                        csr_in(j, 0)
                    for i in range(SCAN_NI):
                        q = i // 8
                        if SCAN_SUB >= 6 and i % 8 == 1 and q + 1 < 4:
                            if q >= 1:
                                p.W("sync", s_gpc, t1d.get(8 * q - 1, 0))
                            dzq.append(p.op("sync", lambda e, c=c, q=q: e.dma_start(
                                zcq[(q + 1) % 2].rearrange("p (a t) -> p a t", a=8),
                                zs_d[8 * (q + 1):8 * (q + 2), :,
                                     c * 256:(c + 1) * 256]
                                .rearrange("a p t -> p a t")), s_z[(q + 1) % 2], 16))
                        if SCAN_SUB >= 6 and i % 8 == 4 and q + 1 < 4:
                            p.W("act", s_z[(q + 1) % 2], dzq[q + 1])
                            p.W("act", s_gpc, t1d.get(8 * q - 1, 0))
                            nzs[q + 1] = p.op("act", lambda e, q=q: e.activation(
                                zcq[(q + 1) % 2], zcq[(q + 1) % 2], AF.Silu),
                                s_act, 1)
                        dbk = i % 2
                        if i == 0:
                            emit_bcast(0)
                            emit_bcast(1)
                        nd1 = d1mm[i]
                        p.W("dve", s_pe, nd1)
                        p.W("dve", s_act, d1eb.get(i - 2, 0))
                        nevac = p.op("dve", lambda e, dbk=dbk, i=i: e.tensor_copy(
                            csb2[i % 2], bank(dbk)), s_dve, 1)
                        d1ev[i] = nevac
                        p.W("act", s_dve, nevac)
                        p.W("act", s_gpc, ncet.get(i - 2, 0))
                        neb = p.op("act", lambda e, i=i: e.activation(
                            ebx2[i % 2], csb2[i % 2], AF.Exp), s_act, 1)
                        d1eb[i] = neb
                        if i + 2 < SCAN_NI:
                            emit_bcast(i + 2)
                        if SCAN_SUB >= 2:
                            for hh_i in range(2):
                                hh = 2 * i + hh_i
                                p.W("dve", s_act, d1le.get((i - 2, hh_i), 0))
                                for stn in range(2):
                                    p.op("dve", lambda e, stn=stn,
                                         hh=hh, hh_i=hh_i, i=i: e.tensor_scalar(
                                        ltm4[i % 2][hh_i][:, stn * 256:(stn + 1) * 256],
                                        csb2[i % 2][:, hh_i * 256:(hh_i + 1) * 256],
                                        csT[:, stn * 64 + hh:stn * 64 + hh + 1],
                                        0.0, ALU.subtract, ALU.min), s_dve, 1)
                                d1lt[(i, hh_i)] = s_dve.n
                                p.W("act", s_dve, d1lt[(i, hh_i)])
                                p.W("act", s_gpc, nga.get((i - 2, hh_i), 0))
                                d1le[(i, hh_i)] = p.op(
                                    "act", lambda e, hh_i=hh_i, i=i: e.activation(
                                        ltx4[i % 2][hh_i], ltm4[i % 2][hh_i],
                                        AF.Exp), s_act, 1)
                                p.W("gp", s_act, neb)
                                p.W("gp", s_xbc, dxb)
                                p.W("gp", s_pe, pe_y.get(i - 2, 0))
                                p.op("gp", lambda e, hh_i=hh_i, i=i:
                                     e.tensor_tensor(
                                    cet4[i % 2][hh_i], cct,
                                    ebx2[i % 2][:, hh_i * 256:(hh_i + 1) * 256],
                                    ALU.mult), s_gpc, 1)
                                ncet[i] = s_gpc.n
                                p.W("gp", s_act, d1le[(i, hh_i)])
                                p.W("gp", s_dve, nm1[-1])
                                p.op("gp", lambda e, hh_i=hh_i, i=i:
                                     e.tensor_tensor(
                                    a1t4[i % 2][hh_i],
                                    ltx4[i % 2][hh_i], m1t, ALU.mult), s_gpc, 1)
                                nga[(i, hh_i)] = s_gpc.n
                        if SCAN_SUB < 3:
                            continue
                        if i == 0:
                            emit_xt(0)
                        if i + 1 < SCAN_NI:
                            emit_xt(i + 1)
                        if SCAN_SUB < 4:
                            continue
                        if i >= 1:
                            emit_post(i - 1)
                        yb = bank(2 + (i % 2), 256)
                        p.W("pe", s_act, xdone[i])
                        p.W("pe", s_dve, xddn[i])
                        p.W("pe", s_gpc, nga.get((i, 1), 0))
                        p.W("pe", s_gpc, ncet.get(i, 0))
                        p.W("pe", s_dve, nt1x.get(i - 2, 0))
                        if c > 0:
                            p.W("pe", s_gpc, stbd.get((c - 1, i), 0))
                        for half, asl in ((0, 0), (1, 1)):
                            r0 = half * 64
                            for stn in range(2):
                                p.op("pe", lambda e, i=i, stn=stn, half=half,
                                     asl=asl, yb=yb, r0=r0: e.matmul(
                                    yb[r0:r0 + 64, :],
                                    xdt[stn][:, i * 128 + half * 64:
                                             i * 128 + half * 64 + 64],
                                    a1t4[i % 2][asl][:, stn * 256:(stn + 1) * 256],
                                    start=(stn == 0), stop=False), s_pe, 1)
                            p.op("pe", lambda e, i=i, half=half, asl=asl, yb=yb,
                                 r0=r0: e.matmul(
                                yb[r0:r0 + 64, :],
                                stb[:, i * 128 + half * 64:
                                    i * 128 + half * 64 + 64],
                                cet4[i % 2][asl], start=False, stop=True), s_pe, 1)
                        npy = s_pe.n
                        pe_y[i] = npy
                        if SCAN_SUB < 5:
                            continue
                        p.W("pe", s_dve, updd.get((c, i - 1) if i >= 1
                                                  else (c - 1, 31), 0))
                        p.W("pe", s_dve, nm1[-1])
                        for stn in range(2):
                            p.op("pe", lambda e, i=i, stn=stn: e.matmul(
                                bank(4)[:, (i % 2) * 128:(i % 2) * 128 + 128],
                                btok[:, stn * 128:(stn + 1) * 128],
                                xdd[stn][:, i * 128:(i + 1) * 128],
                                start=(stn == 0), stop=(stn == 1)), s_pe, 1)
                        s4d[i] = s_pe.n
                    if SCAN_SUB >= 4 and SCAN_NI >= 1:
                        emit_post(SCAN_NI - 1)
                    prev_t1d_end = t1d.get(SCAN_NI - 1, 0)
                    if SCAN_NI < 32:
                        cmax = [(s_pe, s_pe.n), (s_act, s_act.n),
                                (s_dve, s_dve.n), (s_gpc, s_gpc.n),
                                (s_xbc, s_xbc.n), (s_z[0], s_z[0].n),
                                (s_z[1], s_z[1].n), (s_csi[0], s_csi[0].n),
                                (s_csi[1], s_csi[1].n), (s_ynw, s_ynw.n)]
                        prev_t1d_end = t1d.get(SCAN_NI - 1, 0)
                        continue
                    npe_s = 0
                    nsq = {}
                    for kk in range(32):
                        p.W("act", s_gpc, t1d[kk])
                        p.W("act", s_pe, npe_s)
                        nsq[kk] = p.op("act", lambda e, kk=kk: e.activation(
                            sqx[kk % 2], ygb[:, kk * 256:(kk + 1) * 256],
                            AF.Square), s_act, 1)
                        p.W("pe", s_act, nsq[kk])
                        if kk == 0:
                            p.W("pe", s_act, prevev)
                        npe_s = p.op("pe", lambda e, kk=kk: e.matmul(
                            b6[0:1, 0:256], onescol[:], sqx[kk % 2],
                            start=(kk == 0), stop=(kk == 31)), s_pe, 1)
                    p.W("act", s_pe, npe_s)
                    p.op("act", lambda e: e.activation(
                        srms[:, 0:256], b6[0:1, 0:256], AF.Ln, scale=1.0 / DI,
                        bias=epsc), s_act, 1)
                    nrs = p.op("act", lambda e: e.activation(
                        srms[:, 0:256], srms[:, 0:256], AF.Exp, scale=-0.5),
                        s_act, 1)
                    p.W("pe", s_act, nrs)
                    p.W("pe", s_dve, nm1[-1])
                    n = p.op("pe", lambda e: e.matmul(
                        bank(5)[:, 0:256], onesrow[:], srms[:, 0:256],
                        start=True, stop=True), s_pe, 1)
                    p.W("dve", s_pe, n)
                    nrb = p.op("dve", lambda e: e.tensor_copy(
                        rbx, bank(5)[:, 0:256]), s_dve, 1)
                    nyn = {}
                    for kk in range(32):
                        p.W("dve", s_pe, npe_s)
                        p.W("dve", s_dve, nrb)
                        nyn[kk] = p.op("dve", lambda e, kk=kk:
                                       e.scalar_tensor_tensor(
                            ygb[:, kk * 256:(kk + 1) * 256],
                            ygb[:, kk * 256:(kk + 1) * 256],
                            normw[:, kk:kk + 1], rbx, ALU.mult, ALU.mult),
                            s_dve, 1)
                    for g4 in range(4):
                        p.W("sync", s_dve, nyn[g4 * 8 + 7])
                        p.op("sync", lambda e, c=c, g4=g4: e.dma_start(
                            yn_d[g4 * 8:(g4 + 1) * 8, :, c * 256:(c + 1) * 256]
                            .rearrange("a p t -> p a t"),
                            ygb[:, g4 * 8 * 256:(g4 + 1) * 8 * 256]
                            .rearrange("p (a t) -> p a t", a=8)), s_ynw, 16)
                    cmax = [(s_pe, s_pe.n), (s_act, s_act.n), (s_dve, s_dve.n),
                            (s_gpc, s_gpc.n), (s_xbc, s_xbc.n),
                            (s_z[0], s_z[0].n), (s_z[1], s_z[1].n),
                            (s_csi[0], s_csi[0].n), (s_csi[1], s_csi[1].n),
                            (s_ynw, s_ynw.n)]
                return cmax

            def emit_m2(scan_end, ffn1_xo):
                for sm, v in ffn1_xo:
                    p.W("sync", sm, v)
                npf = p.op("sync", lambda e: e.dma_start(
                    x2_d, x1_d), s_pfs[1], 16)
                for eng in ("pe", "act", "dve", "gp", "sync"):
                    for sm, v in scan_end:
                        p.W(eng, sm, v)
                ynl = []
                for kk in range(32):
                    ynl.append(p.op("sync", lambda e, kk=kk: e.dma_start(
                        BIG[:, kk * 2048:(kk + 1) * 2048], yn_d[kk]), s_ynr, 16))
                wdma, ped, dvd, xod = [], [], [], []

                def wout_load(dmt):
                    if dmt >= 2:
                        p.W("gp", s_pe, ped[dmt - 2])
                    wdma.append(p.op("gp", lambda e, dmt=dmt: e.dma_start(
                        woutb[dmt % 2],
                        wout_d[dmt].rearrange("p k h -> p (k h)")),
                        s_wo[dmt % 2], 16))

                wout_load(0)
                wout_load(1)
                for dmt in range(16):
                    p.W("pe", s_wo[dmt % 2], wdma[dmt])
                    p.W("pe", s_ynr, ynl[-1])
                    if dmt >= 2:
                        p.W("pe", s_dve, dvd[dmt - 2])
                    for k in range(32):
                        for tb in range(NTB):
                            p.op("pe", lambda e, dmt=dmt, k=k, tb=tb: e.matmul(
                                bank((dmt % 2) * 4 + tb),
                                woutb[dmt % 2][:, k * 128:(k + 1) * 128],
                                BIG[:, k * 2048 + tb * TB:
                                    k * 2048 + (tb + 1) * TB],
                                start=(k == 0), stop=(k == 31)),
                                s_pe if (k == 31 and tb == NTB - 1) else None)
                    ped.append(s_pe.n)
                    if dmt + 2 < 16:
                        wout_load(dmt + 2)
                    nlast = 0
                    for tb in range(NTB):
                        j = dmt * 4 + tb
                        p.W("dve", s_pe, ped[dmt])
                        if len(xod) >= 2:
                            p.W("dve", s_xg[j % 2], xod[-2])
                        nlast = p.op("dve", lambda e, dmt=dmt, tb=tb, j=j:
                                     e.tensor_copy(
                            xoutb[j % 2], bank((dmt % 2) * 4 + tb)), s_dve, 1)
                        p.W("gp", s_dve, nlast)
                        p.W("gp", s_pfs[1], npf)
                        xod.append(p.op("gp", lambda e, dmt=dmt, tb=tb, j=j:
                                        e.dma_start(
                            x2_d[dmt * 128:(dmt + 1) * 128,
                                 tb * TB:(tb + 1) * TB], xoutb[j % 2],
                            accum_op=ALU.add), s_xg[j % 2], 16))
                    dvd.append(nlast)
                return ped[-1], dvd[-1], [(s_xg[0], s_xg[0].n),
                                          (s_xg[1], s_xg[1].n)]

            def finish_early():
                nf = p.op("sync", lambda e: e.dma_start(
                    xo_d[0:128, :], lnx[0]), s_xou[0], 16)
                p.W("act", s_xou[0], nf)
                p.W("sync", s_xou[0], nf)

            ln1 = emit_ln(u_d, [(s_pe, p1_pe_end),
                                (s_dve, p1_dve_end)] + w2f_ready)
            if STOP <= 2:
                return finish_early()
            ffn1_pe, ffn1_ev, ffn1_xo = emit_ffn(u_d, x1_d, ln1,
                                                s_pfs[1], [])
            if STOP <= 3:
                return finish_early()
            ln2 = emit_ln(x1_d, [(s_pe, ffn1_pe),
                                 (s_dve, ffn1_ev)] + ffn1_xo)
            if STOP <= 4:
                return finish_early()
            m1_pe, m1_dve, m1_sy = emit_m1(ln2)
            if STOP <= 5:
                return finish_early()
            psc, st_init = emit_prescan(m1_dve)
            if STOP <= 6:
                return finish_early()
            scan_end = emit_scan(psc, st_init, m1_pe, m1_sy)
            if STOP <= 7:
                return finish_early()
            m2_pe, m2_dve, m2_xo = emit_m2(scan_end, ffn1_xo)
            if STOP <= 8:
                return finish_early()
            ln3 = emit_ln(x2_d, [(s_pe, m2_pe), (s_dve, m2_dve)] + m2_xo)
            if STOP <= 9:
                return finish_early()
            ffn3_pe, ffn3_ev, ffn3_xo = emit_ffn(x2_d, xo_d, ln3,
                                                s_pfs[3], m2_xo)
            for sm, v in ffn3_xo:
                p.W("act", sm, v)
                p.W("sync", sm, v)

        block = st.enter_context(nc.Block())

        def run_pass(name):
            def f(eng):
                sems = {k: Sem(v, k) for k, v in sem_handles.items()}
                emit(P(name, eng, sems))
            return f

        block.tensor(run_pass("pe"))
        block.scalar(run_pass("act"))
        block.vector(run_pass("dve"))
        block.gpsimd(run_pass("gp"))
        block.sync(run_pass("sync"))

    return nc


_NC_CACHE = {}


def kernel(**inputs):
    u = np.asarray(inputs["L0_attn_u"], dtype=np.float32)
    wi = np.asarray(inputs["L0_attn_in_proj_w"], dtype=np.float32)
    cw = np.asarray(inputs["L0_attn_conv_w"], dtype=np.float32)
    cb = np.asarray(inputs["L0_attn_conv_b"], dtype=np.float32)
    dtb = np.asarray(inputs["L0_attn_dt_bias"], dtype=np.float32)
    alog = np.asarray(inputs["L0_attn_A_log"], dtype=np.float32)
    Dv = np.asarray(inputs["L0_attn_D"], dtype=np.float32)
    nw = np.asarray(inputs["L0_attn_norm_weight"], dtype=np.float32)
    wo = np.asarray(inputs["L0_attn_out_proj_w"], dtype=np.float32)
    W1 = np.asarray(inputs["L0_ffn_W1"], dtype=np.float32)
    W2 = np.asarray(inputs["L0_ffn_W2"], dtype=np.float32)
    W3 = np.asarray(inputs["L0_ffn_W3"], dtype=np.float32)

    w1a = np.ascontiguousarray(W1.T.reshape(NK, 128, NA, 128).transpose(2, 1, 0, 3))
    w3a = np.ascontiguousarray(W3.T.reshape(NK, 128, NA, 128).transpose(2, 1, 0, 3))
    w2q = np.ascontiguousarray(
        W2.T.reshape(64, 128, 2, 1024).transpose(0, 2, 1, 3).reshape(128, 64 * 2048))
    wip = np.zeros((FIN * 128, D), np.float32)
    wip[:8512] = wi
    wina = np.ascontiguousarray(
        wip.T.reshape(NK, 128, FIN, 128).transpose(2, 1, 0, 3))
    wouta = np.ascontiguousarray(
        wo.T.reshape(32, 128, 16, 128).transpose(2, 1, 0, 3))
    convw = np.ascontiguousarray(cw.reshape(34, 128, 4).transpose(1, 0, 2)
                                 .reshape(128, 136))
    convb = np.ascontiguousarray(cb.reshape(34, 128).T)
    dpair = np.ascontiguousarray(
        Dv.reshape(32, 2).T.reshape(2, 1, 32).repeat(64, axis=1).reshape(128, 32))
    normw = np.ascontiguousarray(nw.reshape(32, 128).T)

    shared = dict(w1a=w1a, w3a=w3a, w2q=w2q, wina=wina, wouta=wouta,
                  convw_d=convw, convb_d=convb, dtbias_d=dtb.reshape(64, 1),
                  alog_d=alog.reshape(64, 1), dpair_d=dpair, normw_d=normw)
    in_maps = []
    for c in range(8):
        m = dict(shared)
        m["u"] = np.ascontiguousarray(u[c].T)
        in_maps.append(m)

    nc = _NC_CACHE.get("nc")
    if nc is None:
        nc = build_program()
        _NC_CACHE["nc"] = nc
    res = run_bass_kernel_spmd(nc, in_maps, core_ids=list(range(8)))
    out = np.stack([np.ascontiguousarray(res.results[c]["xo"].T)
                    for c in range(8)], axis=0)
    return out


# version 19 (final)
# speedup vs baseline: 1.0980x; 1.0980x over previous
from contextlib import ExitStack
import numpy as np
import ml_dtypes
import concourse.bass as bass
import concourse.mybir as mybir
from concourse.bass_utils import run_bass_kernel_spmd

F32 = mybir.dt.float32
BF16 = mybir.dt.bfloat16
AF = mybir.ActivationFunctionType
ALU = mybir.AluOpType
BF = ml_dtypes.bfloat16

D, T, HF = 2048, 2048, 8192
NA, NK = 64, 16
NTB, TB = 4, 512
WSUB = 4
DI, NH, CH, NCH = 4096, 64, 256, 8
FIN = 67
EPS = 1e-5
DBG = False
STOP = 10
SCAN_NCH = NCH
SCAN_NI = 32
SCAN_SUB = 6
RSQ = float(1.0 / np.sqrt(8192.0))


def _hadamard(n):
    h = np.array([[1.0]], dtype=np.float64)
    while h.shape[0] < n:
        h = np.block([[h, h], [h, -h]])
    return h


class Sem:
    __slots__ = ("h", "name", "n")

    def __init__(self, h, name):
        self.h, self.name, self.n = h, name, 0

    def inc(self, k):
        self.n += k
        return self.n


class P:

    ENG_SEM = {"dve": "s_dve", "act": "s_act", "gp": "s_gpc"}

    def __init__(self, cur, eng, sems):
        self.cur = cur
        self.eng = eng
        self.s = sems
        self.wm = {}

    def W(self, eng, sem, val):
        if val is None or val <= 0:
            return
        key = (eng, sem.name)
        if self.wm.get(key, 0) >= val:
            return
        self.wm[key] = val
        if self.cur == eng:
            self.eng.wait_ge(sem.h, val)

    def op(self, eng, build, sem=None, k=1):
        esem_name = self.ENG_SEM.get(eng)
        if esem_name is not None and k != 16:
            esem = self.s[esem_name]
            assert sem is None or sem is esem
            self.W(eng, esem, esem.n)
            sem = esem
            k = 1
        if self.cur == eng:
            ins = build(self.eng)
            if sem is not None:
                ins.then_inc(sem.h, k)
        return sem.inc(k) if sem is not None else None


def build_program():
    nc = bass.Bass("TRN2", target_bir_lowering=False)

    u_d = nc.dram_tensor("u", [D, T], F32, kind="ExternalInput").ap()
    w1_d = nc.dram_tensor("w1a", [NA, 128, NK, 128], F32, kind="ExternalInput").ap()
    w3_d = nc.dram_tensor("w3a", [NA, 128, NK, 128], F32, kind="ExternalInput").ap()
    w2q_d = nc.dram_tensor("w2q", [128, 64 * D], F32, kind="ExternalInput").ap()
    win_d = nc.dram_tensor("wina", [FIN, 128, NK, 128], F32, kind="ExternalInput").ap()
    wout_d = nc.dram_tensor("wouta", [16, 128, 32, 128], F32, kind="ExternalInput").ap()
    convw_d = nc.dram_tensor("convw_d", [128, 136], F32, kind="ExternalInput").ap()
    convb_d = nc.dram_tensor("convb_d", [128, 34], F32, kind="ExternalInput").ap()
    dtb_d = nc.dram_tensor("dtbias_d", [64, 1], F32, kind="ExternalInput").ap()
    alog_d = nc.dram_tensor("alog_d", [64, 1], F32, kind="ExternalInput").ap()
    dpair_d = nc.dram_tensor("dpair_d", [128, 32], F32, kind="ExternalInput").ap()
    nw_d = nc.dram_tensor("normw_d", [128, 32], F32, kind="ExternalInput").ap()
    xo_d = nc.dram_tensor("xo", [D, T], F32, kind="ExternalOutput").ap()

    onesrow_d = nc.inline_tensor(np.ones((1, 128), np.float32), name="onesrowc")
    onescol_d = nc.inline_tensor(np.ones((128, 1), np.float32), name="onescolc")
    idf_d = nc.inline_tensor(np.eye(128, dtype=np.float32), name="idfc")
    idb_d = nc.inline_tensor(np.eye(128, dtype=np.float32).astype(BF), name="idbc")
    k2_d = nc.inline_tensor(np.kron(_hadamard(64), np.eye(2)).astype(BF), name="k2cc")
    h128_d = nc.inline_tensor(_hadamard(128).astype(BF), name="h128cc")
    eps_d = nc.inline_tensor(np.full((1, 1), EPS, np.float32), name="epscc")
    _mk = np.zeros((128, 512), np.float32)
    for _stn in range(2):
        for _p in range(128):
            _mk[_p, _stn * 256 + _stn * 128 + _p:(_stn + 1) * 256] = 1.0
    maskc_d = nc.inline_tensor(_mk, name="maskcc")

    if DBG:
        dbg_pre = nc.dram_tensor("dbg_pre", [3, 64, T], F32).ap()
        dbg_yg = nc.dram_tensor("dbg_yg", [128, 8192], BF16).ap()
        dbg_st = nc.dram_tensor("dbg_st", [128, 4096], F32).ap()
        dbg_bc = nc.dram_tensor("dbg_bc", [2, 128, 256], BF16).ap()
        dbg_m1t = nc.dram_tensor("dbg_m1t", [128, 512], F32).ap()
        dbg_xdt = nc.dram_tensor("dbg_xdt", [2, 128, 4096], BF16).ap()
        dbg_xdd = nc.dram_tensor("dbg_xdd", [2, 128, 4096], BF16).ap()
        dbg_dd = nc.dram_tensor("dbg_dd", [2, 64, 256], F32).ap()
        dbg_tt = nc.dram_tensor("dbg_tt", [3, 128, 128], F32).ap()
    x1_d = nc.dram_tensor("x1s", [D, T], F32).ap()
    x2_d = nc.dram_tensor("x2s", [D, T], F32).ap()
    hid2_d = nc.dram_tensor("hid2s", [NA, NTB, 128, TB], BF16).ap()
    w2mid_d = nc.dram_tensor("w2mids", [NA, 128, D], BF16).ap()
    w2f_d = nc.dram_tensor("w2fs", [NA, 128, D], BF16).ap()
    zs_d = nc.dram_tensor("zss", [32, 128, T], BF16).ap()
    xbc_d = nc.dram_tensor("xbcs", [34, 128, T], BF16).ap()
    yn_d = nc.dram_tensor("yns", [32, 128, T], BF16).ap()
    csr_d = nc.dram_tensor("csrs", [64, T], F32).ap()

    st = ExitStack()
    with st:
        BIG = st.enter_context(nc.sbuf_tensor("BIG", [128, 65536], BF16)).ap()
        S1 = st.enter_context(nc.sbuf_tensor("S1", [128, 12288], BF16)).ap()
        S2 = st.enter_context(nc.sbuf_tensor("S2", [128, 12288], BF16)).ap()

        def fv(ap, c0, nf, p=128):
            out = ap[:, c0:c0 + 2 * nf].bitcast(F32)
            return out if p == 128 else out[0:p, :]

        xln = BIG[:, 0:32768]
        hid2tb = [BIG[:, 32768:65536], BIG[:, 0:32768]]
        w1b = [S1[:, 0:2048], S1[:, 2048:4096]]
        w3b = [S1[:, 4096:6144], S1[:, 6144:8192]]
        hidb = [S2[:, 8192:10240], S2[:, 10240:12288]]
        sgb = [fv(S2, 0, 2048), fv(S2, 4096, 2048)]
        w2fb = [S2[:, 0:4096], S2[:, 4096:8192]]
        xinb = [fv(S2, 8192, 512), fv(S2, 9216, 512)]
        xoutb = [fv(S2, 10240, 512), fv(S2, 11264, 512)]
        winb = [S1[:, 0:2048], S1[:, 2048:4096]]
        m1st = [S1[:, 4096:6144], S1[:, 6144:8192]]
        woutb = [S1[:, 0:4096], S1[:, 4096:8192]]
        lnx = [fv(S1, 0, 2048), fv(S1, 4096, 2048)]
        lnsq = [fv(S1, 8192, 2048), fv(S2, 8192, 2048)]
        mb_sb = fv(S2, 0, 2048)
        rb_sb = fv(S2, 4096, 2048)
        p1in = [S1[:, 0:4096], S1[:, 4096:8192]]
        p1st = [S2[:, 0:512], S2[:, 512:1024], S2[:, 1024:1536], S2[:, 1536:2048]]
        p1m2 = [S1[:, 8192:10240], S1[:, 10240:12288]]
        p1o2 = [S2[:, 2048:4096], S2[:, 4096:6144]]

        xbcc = BIG[:, 0:8806]
        zcq = [BIG[:, 8840:10888], BIG[:, 10888:12936]]
        bct = BIG[:, 13960:14216]
        cct = BIG[:, 14216:14472]
        xdt = [BIG[:, 14472:18568], BIG[:, 18568:22664]]
        xdd = [BIG[:, 22664:26760], BIG[:, 26760:30856]]
        ygb = BIG[:, 30856:39048]
        stf = fv(BIG, 39048, 4096)
        stb = BIG[:, 47240:51336]
        daf = fv(BIG, 51336, 2048, p=64)
        dtf = fv(BIG, 55432, 2048, p=64)
        csf = fv(BIG, 59528, 2048, p=64)
        btok = BIG[:, 63624:63880]
        row0 = fv(BIG, 40000, 2048, p=1)
        row1 = fv(BIG, 44096, 2048, p=1)
        row2 = fv(BIG, 48192, 2048, p=1)
        srms = fv(S1, 0, 512, p=1)

        def sbt(name, shape, dtype):
            return st.enter_context(nc.sbuf_tensor(name, shape, dtype)).ap()

        maskc = sbt("maskc", [128, 512], F32)
        onesrow = sbt("onesrow", [1, 128], F32)
        epsc = sbt("epsc", [1, 1], F32)
        onescol = sbt("onescol", [128, 1], F32)
        idf = sbt("idf", [128, 128], F32)
        idb = sbt("idb", [128, 128], BF16)
        k2c = sbt("k2c", [128, 128], BF16)
        h128c = sbt("h128c", [128, 128], BF16)
        convw = sbt("convw", [128, 136], F32)
        convb = sbt("convb", [128, 34], F32)
        dtbias = sbt("dtbias", [64, 1], F32)
        alogc = sbt("alogc", [64, 1], F32)
        acol = sbt("acol", [64, 1], F32)
        dpair = sbt("dpair", [128, 32], F32)
        normw = sbt("normw", [128, 32], F32)
        zero64 = sbt("zero64", [64, 256], F32)
        csT = sbt("csT", [128, 128], F32)
        dtT = sbt("dtT", [128, 128], F32)
        ddT = sbt("ddT", [128, 128], F32)
        ddb = sbt("ddb", [64, 256], F32)
        decayb = sbt("decayb", [64, 256], F32)
        cdcol = sbt("cdcol", [64, 1], F32)
        cdrow = sbt("cdrow", [1, 64], F32)
        cdecrep = sbt("cdecrep", [128, 64], F32)
        m1t = sbt("m1t", [128, 512], F32)
        decT = sbt("decT", [128, 128], F32)
        a1t4 = [[sbt("a1t0", [128, 512], BF16), sbt("a1t1", [128, 512], BF16)],
                [sbt("a1t2", [128, 512], BF16), sbt("a1t3", [128, 512], BF16)]]
        cet4 = [[sbt("cet0", [128, 256], BF16), sbt("cet1", [128, 256], BF16)],
                [sbt("cet2", [128, 256], BF16), sbt("cet3", [128, 256], BF16)]]
        t1x = sbt("t1x", [128, 256], F32)
        t2x = sbt("t2x", [128, 128], F32)
        cvt = [sbt("cvt0", [128, 256], F32), sbt("cvt1", [128, 256], F32)]
        rbx = sbt("rbx", [128, 256], F32)
        sqx = [sbt("sqx0", [128, 256], F32), sbt("sqx1", [128, 256], F32)]
        xsr = S2[:, 0:4102].bitcast(F32)
        cvf2 = [S2[:, 4352:8448].bitcast(F32), S1[:, 8192:12288].bitcast(F32)]
        xall = S1[:, 1024:9216]
        ebx2 = [fv(S2, 0, 512), fv(S2, 1024, 512)]
        ltm4 = [[fv(S2, 2048, 512), fv(S2, 3072, 512)],
                [fv(S2, 8192, 512), fv(S2, 9216, 512)]]
        ltx4 = [[fv(S2, 4096, 512), fv(S2, 5120, 512)],
                [fv(S2, 10240, 512), fv(S2, 11264, 512)]]
        csrg = [BIG[0:1, k * 2048:(k + 1) * 2048].bitcast(F32)
                for k in range(4)]
        csb2 = [fv(S2, 6144, 512), fv(S2, 7168, 512)]

        PS = st.enter_context(nc.psum_tensor("PSf", [128, 4096], F32)).ap()
        PSB = PS[:, 3584:4096].bitcast(BF16)

        def bank(i, w=512):
            return PS[:, i * 512:i * 512 + w]

        sem_handles = {}
        for name in ("s_c", "s_pe", "s_act", "s_dve", "s_gpc",
                     "p1i0", "p1i1", "p1w0", "p1w1", "p1w2", "p1w3",
                     "p1r0", "p1r1", "p1o0", "p1o1",
                     "lnx0", "lnx1", "w130", "w131", "hdw0", "hdw1", "hdr",
                     "w2f0", "w2f1", "xin0", "xin1", "xou0", "xou1",
                     "win0", "win1", "m1w0", "m1w1",
                     "xbc", "z0", "z1", "ynw", "ynr", "wo0", "wo1",
                     "cso", "csi0", "csi1", "csi2", "csi3",
                     "pf1", "pf3", "hdb", "xga", "xgb"):
            sem_handles[name] = st.enter_context(nc.semaphore(name))

        def emit(p: P):
            S = p.s
            s_c, s_pe, s_act, s_dve, s_gpc = (
                S[n] for n in ("s_c", "s_pe", "s_act", "s_dve", "s_gpc"))
            s_p1i = [S["p1i0"], S["p1i1"]]
            s_p1w = [S["p1w0"], S["p1w1"], S["p1w2"], S["p1w3"]]
            s_p1r = [S["p1r0"], S["p1r1"]]
            s_p1o = [S["p1o0"], S["p1o1"]]
            s_lnx = [S["lnx0"], S["lnx1"]]
            s_w13 = [S["w130"], S["w131"]]
            s_hdw = [S["hdw0"], S["hdw1"]]
            s_hdr2 = [S["hdr"], S["hdb"]]
            s_w2f = [S["w2f0"], S["w2f1"]]
            s_xin = [S["xin0"], S["xin1"]]
            s_xou = [S["xou0"], S["xou1"]]
            s_win = [S["win0"], S["win1"]]
            s_m1w = [S["m1w0"], S["m1w1"]]
            s_xbc = S["xbc"]
            s_cso = S["cso"]
            s_pfs = {1: S["pf1"], 3: S["pf3"]}
            s_xg = [S["xga"], S["xgb"]]
            s_csi = [S["csi0"], S["csi1"], S["csi2"], S["csi3"]]
            s_z = [S["z0"], S["z1"]]
            s_ynw = S["ynw"]
            s_ynr = S["ynr"]
            s_wo = [S["wo0"], S["wo1"]]

            for dst, src in ((onesrow, onesrow_d.ap()), (onescol, onescol_d.ap()),
                             (idf, idf_d.ap()), (idb, idb_d.ap()),
                             (k2c, k2_d.ap()), (h128c, h128_d.ap()),
                             (epsc, eps_d.ap()), (maskc, maskc_d.ap()),
                             (convw, convw_d), (convb, convb_d),
                             (dtbias, dtb_d), (alogc, alog_d), (dpair, dpair_d),
                             (normw, nw_d)):
                p.op("sync", lambda e, d=dst, s=src: e.dma_start(d, s), s_c, 16)
            nconst = s_c.n
            p.op("gp", lambda e: e.memset(zero64, 0.0), s_gpc, 1)
            gp_init = s_gpc.n

            p.W("gp", s_c, nconst)
            pe_ch, dv_sl = [], []
            slot_w = [0, 0, 0, 0]
            for j in range(32):
                if j >= 2:
                    p.W("gp", s_pe, pe_ch[j - 2])
                dn = p.op("gp", lambda e, j=j: e.dma_start(
                    p1in[j % 2], w2q_d[:, j * 4096:(j + 1) * 4096]),
                    s_p1i[j % 2], 16)
                p.W("pe", s_p1i[j % 2], dn)
                p.W("pe", s_c, nconst)
                if j >= 1:
                    p.W("pe", s_dve, dv_sl[-1])
                base_pe = s_pe.n
                for m in range(8):
                    p.op("pe", lambda e, j=j, m=m: e.matmul(
                        bank(m), k2c[:], p1in[j % 2][:, m * 512:(m + 1) * 512],
                        start=True, stop=True), s_pe, 1)
                pe_ch.append(s_pe.n)
                for m in range(8):
                    sl = j * 8 + m
                    p.W("dve", s_pe, base_pe + m + 1)
                    if sl >= 4:
                        p.W("dve", s_p1w[sl % 4], slot_w[sl % 4])
                    dv_sl.append(p.op("dve", lambda e, j=j, m=m, sl=sl:
                                      e.tensor_scalar(
                        p1st[sl % 4], bank(m), RSQ, None, ALU.mult), s_dve, 1))
                    p.W("sync", s_dve, dv_sl[sl])
                    bcol, half = 4 * j + m // 2, m % 2
                    dst = bass.AP(w2mid_d.tensor, bcol * 2048 + half * 512,
                                  [[262144, 64], [1024, 2], [1, 512]])
                    slot_w[sl % 4] = p.op("sync", lambda e, sl=sl, dst=dst:
                                          e.dma_start(
                        dst, p1st[sl % 4]), s_p1w[sl % 4], 16)
            p1w_tot = [s.n for s in s_p1w]
            in2, pe2, dv2, o2 = [], [], [], []
            for a in range(NA):
                for t in range(4):
                    p.W("sync", s_p1w[t], p1w_tot[t])
                if a >= 2:
                    p.W("sync", s_pe, pe2[a - 2])
                in2.append(p.op("sync", lambda e, a=a: e.dma_start(
                    p1m2[a % 2], w2mid_d[a]), s_p1r[a % 2], 16))
                p.W("pe", s_p1r[a % 2], in2[a])
                if a >= 2:
                    p.W("pe", s_dve, dv2[a - 2])
                for m in range(4):
                    p.op("pe", lambda e, a=a, m=m: e.matmul(
                        bank((a % 2) * 4 + m), h128c[:],
                        p1m2[a % 2][:, m * 512:(m + 1) * 512],
                        start=True, stop=True), s_pe, 1)
                pe2.append(s_pe.n)
                p.W("dve", s_pe, pe2[a])
                if a >= 2:
                    p.W("dve", s_p1o[a % 2], o2[a - 2])
                for m in range(4):
                    p.op("dve", lambda e, a=a, m=m: e.tensor_copy(
                        p1o2[a % 2][:, m * 512:(m + 1) * 512],
                        bank((a % 2) * 4 + m)), s_dve, 1)
                dv2.append(s_dve.n)
                p.W("sync", s_dve, dv2[a])
                o2.append(p.op("sync", lambda e, a=a: e.dma_start(
                    w2f_d[a], p1o2[a % 2]), s_p1o[a % 2], 16))
            w2f_ready = [(s_p1o[0], s_p1o[0].n), (s_p1o[1], s_p1o[1].n),
                         (s_p1w[0], p1w_tot[0]), (s_p1w[1], p1w_tot[1]),
                         (s_p1w[2], p1w_tot[2]), (s_p1w[3], p1w_tot[3]),
                         (s_p1r[0], s_p1r[0].n), (s_p1r[1], s_p1r[1].n)]
            p1_pe_end = pe2[-1]
            p1_dve_end = dv2[-1]

            def emit_ln(src_d, waits):
                for eng in ("act", "pe", "dve", "gp"):
                    for sm, v in waits:
                        p.W(eng, sm, v)
                ld1, sqd, per = [], [], []
                for k in range(NK):
                    if k >= 2:
                        p.W("act", s_pe, per[k - 2])
                    ld1.append(p.op("act", lambda e, k=k: e.dma_start(
                        lnx[k % 2], src_d[k * 128:(k + 1) * 128, :]),
                        s_lnx[k % 2], 16))
                    p.W("act", s_lnx[k % 2], ld1[k])
                    sqd.append(p.op("act", lambda e, k=k: e.activation(
                        lnsq[k % 2], lnx[k % 2], AF.Square), s_act, 1))
                    p.W("pe", s_lnx[k % 2], ld1[k])
                    p.W("pe", s_act, sqd[k])
                    for m in range(4):
                        p.op("pe", lambda e, k=k, m=m: e.matmul(
                            PS[0:1, m * 512:(m + 1) * 512], onescol[:],
                            lnx[k % 2][:, m * 512:(m + 1) * 512],
                            start=(k == 0), stop=(k == NK - 1)), s_pe, 1)
                    for m in range(4):
                        p.op("pe", lambda e, k=k, m=m: e.matmul(
                            PS[0:1, 2048 + m * 512:2048 + (m + 1) * 512],
                            onescol[:],
                            lnsq[k % 2][:, m * 512:(m + 1) * 512],
                            start=(k == 0), stop=(k == NK - 1)), s_pe, 1)
                    per.append(s_pe.n)
                p.W("dve", s_pe, per[-1])
                p.op("dve", lambda e: e.tensor_scalar(row0, PS[0:1, 0:2048],
                                                      1.0 / D, None, ALU.mult),
                     s_dve, 1)
                p.op("dve", lambda e: e.tensor_scalar(row1, PS[0:1, 2048:4096],
                                                      1.0 / D, None, ALU.mult),
                     s_dve, 1)
                p.op("dve", lambda e: e.tensor_tensor(row2, row0, row0, ALU.mult),
                     s_dve, 1)
                nv = p.op("dve", lambda e: e.tensor_sub(row1, row1, row2), s_dve, 1)
                p.W("act", s_dve, nv)
                p.op("act", lambda e: e.activation(row1, row1, AF.Ln, bias=epsc),
                     s_act, 1)
                nr = p.op("act", lambda e: e.activation(row1, row1, AF.Exp,
                                                        scale=-0.5), s_act, 1)
                p.W("pe", s_act, nr)
                for m in range(4):
                    p.op("pe", lambda e, m=m: e.matmul(
                        PS[:, m * 512:(m + 1) * 512], onesrow[:],
                        row0[:, m * 512:(m + 1) * 512], start=True, stop=True),
                        s_pe, 1)
                for m in range(4):
                    p.op("pe", lambda e, m=m: e.matmul(
                        PS[:, 2048 + m * 512:2048 + (m + 1) * 512], onesrow[:],
                        row1[:, m * 512:(m + 1) * 512], start=True, stop=True),
                        s_pe, 1)
                nbc2 = s_pe.n
                p.W("dve", s_pe, nbc2)
                p.op("dve", lambda e: e.tensor_copy(mb_sb, PS[:, 0:2048]), s_dve, 1)
                mbrb = p.op("dve", lambda e: e.tensor_copy(rb_sb, PS[:, 2048:4096]),
                            s_dve, 1)
                ld2, gpd, apd = [], [], []
                for k in range(NK):
                    if k >= 2:
                        p.W("act", s_gpc, gpd[k - 2])
                    ld2.append(p.op("act", lambda e, k=k: e.dma_start(
                        lnx[k % 2], src_d[k * 128:(k + 1) * 128, :]),
                        s_lnx[k % 2], 16))
                    p.W("gp", s_lnx[k % 2], ld2[k])
                    p.W("gp", s_dve, mbrb)
                    if k >= 2:
                        p.W("gp", s_dve, apd[k - 2])
                    gpd.append(p.op("gp", lambda e, k=k: e.tensor_sub(
                        lnsq[k % 2], lnx[k % 2], mb_sb), s_gpc, 1))
                    p.W("dve", s_gpc, gpd[k])
                    apd.append(p.op("dve", lambda e, k=k: e.tensor_mul(
                        xln[:, k * T:(k + 1) * T], lnsq[k % 2], rb_sb), s_dve, 1))
                return apd[-1], s_gpc.n

            def emit_ffn(src_d, dst_d, ln_done, pf_sem, pf_gates):
                ln_dve, ln_gpc = ln_done
                for sm, v in pf_gates:
                    p.W("sync", sm, v)
                npf = p.op("sync", lambda e: e.dma_start(
                    dst_d, src_d), pf_sem, 16)
                wdma, peg, pel, sil, mul, hw = [], [], [], [], [], []
                for a in range(NA):
                    if a >= 2:
                        p.W("gp", s_pe, pel[a - 2])
                    else:
                        p.W("gp", s_gpc, ln_gpc)
                    p.op("gp", lambda e, a=a: e.dma_start(
                        w1b[a % 2], w1_d[a].rearrange("p k h -> p (k h)")),
                        s_w13[a % 2], 16)
                    wdma.append(p.op("gp", lambda e, a=a: e.dma_start(
                        w3b[a % 2], w3_d[a].rearrange("p k h -> p (k h)")),
                        s_w13[a % 2], 16))
                    p.W("pe", s_w13[a % 2], wdma[a])
                    p.W("pe", s_dve, ln_dve)
                    if a >= 1:
                        p.W("pe", s_act, sil[a - 1])
                    for k in range(NK):
                        for tb in range(NTB):
                            p.op("pe", lambda e, a=a, k=k, tb=tb: e.matmul(
                                bank(tb), w1b[a % 2][:, k * 128:(k + 1) * 128],
                                xln[:, k * T + tb * TB:k * T + (tb + 1) * TB],
                                start=(k == 0), stop=(k == NK - 1)),
                                s_pe if (k == NK - 1 and tb == NTB - 1) else None)
                    peg.append(s_pe.n)
                    if a >= 1:
                        p.W("pe", s_dve, mul[a - 1])
                    for k in range(NK):
                        for tb in range(NTB):
                            p.op("pe", lambda e, a=a, k=k, tb=tb: e.matmul(
                                bank(4 + tb),
                                w3b[a % 2][:, k * 128:(k + 1) * 128],
                                xln[:, k * T + tb * TB:k * T + (tb + 1) * TB],
                                start=(k == 0), stop=(k == NK - 1)),
                                s_pe if (k == NK - 1 and tb == NTB - 1) else None)
                    pel.append(s_pe.n)
                    p.W("act", s_pe, peg[a])
                    if a >= 1:
                        p.W("act", s_dve, mul[a - 1])
                    sil.append(p.op("act", lambda e, a=a: e.activation(
                        sgb[a % 2], PS[:, 0:2048], AF.Silu), s_act, 1))
                    p.W("dve", s_pe, pel[a])
                    p.W("dve", s_act, sil[a])
                    if a >= 2:
                        p.W("dve", s_hdw[a % 2], hw[a - 2])
                    mul.append(p.op("dve", lambda e, a=a: e.tensor_tensor(
                        hidb[a % 2], sgb[a % 2], PS[:, 2048:4096], ALU.mult),
                        s_dve, 1))
                    p.W("sync", s_dve, mul[a])
                    hw.append(p.op("sync", lambda e, a=a: e.dma_start(
                        hid2_d[a].rearrange("tb p t -> p tb t"),
                        hidb[a % 2].rearrange("p (tb t) -> p tb t", tb=NTB)),
                        s_hdw[a % 2], 16))
                hid_all = [(s_hdw[0], s_hdw[0].n), (s_hdw[1], s_hdw[1].n)]
                htb, wfd, pea, evj, xoj, xij = [], [], [], [], [], []

                def w2f_batch(wb):
                    bi_w, sub_w = divmod(wb, NA // WSUB)
                    half_w = bi_w % 2
                    a0 = sub_w * WSUB
                    if wb >= 2:
                        p.W("sync", s_pe, pea[-1])
                    else:
                        p.W("sync", s_dve, mul[-1])
                    for sm, v in w2f_ready:
                        p.W("sync", sm, v)
                    wfd.append(p.op("sync", lambda e, a0=a0, half_w=half_w, wb=wb:
                                    e.dma_start(
                        w2fb[wb % 2].rearrange("p (a d) -> p a d", a=WSUB),
                        w2f_d[a0:a0 + WSUB, :,
                              half_w * 1024:(half_w + 1) * 1024]
                        .rearrange("a p d -> p a d")), s_w2f[wb % 2], 16))

                w2f_batch(0)
                w2f_batch(1)
                nwb = 2
                for tb in range(NTB):
                    for sm, v in hid_all:
                        p.W("act", sm, v)
                    if tb >= 2:
                        p.W("act", s_pe, pea[(tb - 1) * 2 * NA - 1])
                    for grp in range(8):
                        p.op("act", lambda e, tb=tb, grp=grp: e.dma_start(
                            hid2tb[tb % 2][:, grp * 8 * TB:(grp + 1) * 8 * TB]
                            .rearrange("p (a t) -> p a t", a=8),
                            hid2_d[grp * 8:(grp + 1) * 8, tb]
                            .rearrange("a p t -> p a t")), s_hdr2[tb % 2], 16)
                    htb.append(s_hdr2[tb % 2].n)
                    for half in range(2):
                        bi = tb * 2 + half
                        p.W("pe", s_hdr2[tb % 2], htb[tb])
                        if bi > 0:
                            p.W("pe", s_dve, evj[bi * 8 - 1])
                        for a in range(NA):
                            if a % WSUB == 0 and not (bi == 0 and a == 0):
                                if nwb < 2 * NA:
                                    w2f_batch(nwb)
                                    nwb += 1
                            wb = bi * (NA // WSUB) + a // WSUB
                            p.W("pe", s_w2f[wb % 2], wfd[wb])
                            for dmt in range(8):
                                p.op("pe", lambda e, a=a, dmt=dmt, tb=tb, wb=wb:
                                     e.matmul(
                                    bank(dmt),
                                    w2fb[wb % 2][:, (a % WSUB) * 1024 + dmt * 128:
                                                 (a % WSUB) * 1024 + dmt * 128 + 128],
                                    hid2tb[tb % 2][:, a * TB:(a + 1) * TB],
                                    start=(a == 0), stop=(a == NA - 1)),
                                    s_pe if dmt == 7 else None)
                            pea.append(s_pe.n)
                        for dmt in range(8):
                            j = bi * 8 + dmt
                            dm0 = half * 8 + dmt
                            p.W("dve", s_pe, pea[-1])
                            if j < 2:
                                for sm, v in hid_all:
                                    p.W("dve", sm, v)
                            if j >= 2:
                                p.W("dve", s_xg[j % 2], xoj[j - 2])
                            evj.append(p.op("dve", lambda e, dmt=dmt, j=j:
                                            e.tensor_scalar(
                                xoutb[j % 2], bank(dmt), 0.5, None,
                                ALU.mult), s_dve, 1))
                            p.W("gp", s_dve, evj[j])
                            p.W("gp", pf_sem, npf)
                            xoj.append(p.op("gp", lambda e, dm0=dm0, tb=tb, j=j:
                                            e.dma_start(
                                dst_d[dm0 * 128:(dm0 + 1) * 128,
                                      tb * TB:(tb + 1) * TB], xoutb[j % 2],
                                accum_op=ALU.add), s_xg[j % 2], 16))
                return (pea[-1], evj[-1],
                        [(s_xg[0], s_xg[0].n), (s_xg[1], s_xg[1].n)])

            def emit_m1(ln_done):
                ln_dve, ln_gpc = ln_done
                wdma, pef, dvf, syf = [], [], [], []
                nzp = p.op("dve", lambda e: e.memset(xsr[:, 0:3], 0.0), s_dve, 1)
                nslp = {}
                evd = {}
                for f in range(FIN):
                    if f >= 2:
                        p.W("gp", s_pe, pef[f - 2])
                    else:
                        p.W("gp", s_gpc, ln_gpc)
                    wdma.append(p.op("gp", lambda e, f=f: e.dma_start(
                        winb[f % 2], win_d[f].rearrange("p k h -> p (k h)")),
                        s_win[f % 2], 16))
                    p.W("pe", s_win[f % 2], wdma[f])
                    p.W("pe", s_dve, ln_dve)
                    if f >= 2:
                        p.W("pe", evd[f - 2][0], evd[f - 2][1])
                    for k in range(NK):
                        for tb in range(NTB):
                            p.op("pe", lambda e, f=f, k=k, tb=tb: e.matmul(
                                bank((f % 2) * 4 + tb),
                                winb[f % 2][:, k * 128:(k + 1) * 128],
                                xln[:, k * T + tb * TB:k * T + (tb + 1) * TB],
                                start=(k == 0), stop=(k == NK - 1)),
                                s_pe if (k == NK - 1 and tb == NTB - 1) else None)
                    pef.append(s_pe.n)
                    if f == 66:
                        p.W("dve", s_pe, pef[f])
                        p.op("dve", lambda e, f=f: e.tensor_copy(
                            daf, PS[0:64, (f % 2) * 2048:(f % 2) * 2048 + 2048]),
                            s_dve, 1)
                        dvf.append(s_dve.n)
                        evd[f] = (s_dve, s_dve.n)
                        syf.append(None)
                    elif f < 32:
                        p.W("act", s_pe, pef[f])
                        if f >= 2 and syf[f - 2] is not None:
                            p.W("act", s_m1w[f % 2], syf[f - 2])
                        nz = p.op("act", lambda e, f=f: e.activation(
                            m1st[f % 2],
                            PS[:, (f % 2) * 2048:(f % 2) * 2048 + 2048],
                            AF.Silu), s_act, 1)
                        evd[f] = (s_act, nz)
                        p.W("sync", s_act, nz)
                        syf.append(p.op("sync", lambda e, f=f: e.dma_start(
                            zs_d[f], m1st[f % 2]), s_m1w[f % 2], 16))
                    else:
                        tile = f - 32
                        cvf = cvf2[f % 2]
                        p.W("dve", s_pe, pef[f])
                        p.op("dve", lambda e, f=f: e.tensor_copy(
                            xsr[:, 3:2051],
                            PS[:, (f % 2) * 2048:(f % 2) * 2048 + 2048]), s_dve, 1)
                        p.W("dve", s_act, nslp.get(f % 2, 0))
                        p.op("dve", lambda e, tile=tile, cvf=cvf: e.tensor_scalar(
                            cvf, xsr[:, 0:2048],
                            convw[:, tile * 4:tile * 4 + 1], None, ALU.mult),
                            s_dve, 1)
                        for kk in range(1, 4):
                            p.op("dve", lambda e, tile=tile, kk=kk, cvf=cvf:
                                 e.scalar_tensor_tensor(
                                cvf, xsr[:, kk:kk + 2048],
                                convw[:, tile * 4 + kk:tile * 4 + kk + 1], cvf,
                                ALU.mult, ALU.add), s_dve, 1)
                        dvf.append(s_dve.n)
                        evd[f] = (s_dve, s_dve.n)
                        p.W("act", s_dve, dvf[-1])
                        if f >= 2 and syf[f - 2] is not None:
                            p.W("act", s_m1w[f % 2], syf[f - 2])
                        nslp[f % 2] = p.op("act", lambda e, f=f, tile=tile,
                                           cvf=cvf: e.activation(
                            m1st[f % 2], cvf, AF.Silu,
                            bias=convb[:, tile:tile + 1]), s_act, 1)
                        p.W("sync", s_act, nslp[f % 2])
                        syf.append(p.op("sync", lambda e, tile=tile, f=f:
                                        e.dma_start(
                            xbc_d[tile], m1st[f % 2]), s_m1w[f % 2], 16))
                return pef[-1], dvf[-1], [(s_m1w[0], s_m1w[0].n),
                                          (s_m1w[1], s_m1w[1].n)]

            def emit_prescan(m1_dve):
                p.W("gp", s_dve, m1_dve)
                p.op("gp", lambda e: e.memset(stf, 0.0), s_gpc, 1)
                st_init = p.op("gp", lambda e: e.memset(stb, 0.0), s_gpc, 1)
                p.W("act", s_c, nconst)
                na = p.op("act", lambda e: e.activation(acol, alogc, AF.Exp),
                          s_act, 1)
                p.W("dve", s_act, na)
                p.W("dve", s_dve, m1_dve)
                p.op("dve", lambda e: e.tensor_scalar(acol, acol, -1.0, None,
                                                      ALU.mult), s_dve, 1)
                p.op("dve", lambda e: e.tensor_scalar(dtf, daf, dtbias, None,
                                                      ALU.add), s_dve, 1)
                p.op("dve", lambda e: e.tensor_scalar(csf, dtf, -1.0, None,
                                                      ALU.mult), s_dve, 1)
                nv = p.op("dve", lambda e: e.tensor_tensor(csf, dtf, csf,
                                                           ALU.max), s_dve, 1)
                p.W("act", s_dve, nv)
                p.op("act", lambda e: e.activation(csf, csf, AF.Exp, scale=-1.0),
                     s_act, 1)
                na = p.op("act", lambda e: e.activation(
                    csf, csf, AF.Ln, bias=onescol[0:64, :]), s_act, 1)
                p.W("dve", s_act, na)
                p.op("dve", lambda e: e.tensor_scalar(dtf, dtf, 0.0, None, ALU.max),
                     s_dve, 1)
                p.op("dve", lambda e: e.tensor_tensor(dtf, dtf, csf, ALU.add),
                     s_dve, 1)
                p.op("dve", lambda e: e.tensor_scalar(daf, dtf, acol, None,
                                                      ALU.mult), s_dve, 1)
                p.W("dve", s_gpc, gp_init)
                nv = None
                for c in range(NCH):
                    nv = p.op("dve", lambda e, c=c: e.tensor_tensor_scan(
                        csf[:, c * CH:(c + 1) * CH], daf[:, c * CH:(c + 1) * CH],
                        zero64, 0.0, ALU.add, ALU.add), s_dve, 1)
                if DBG:
                    p.W("sync", s_dve, nv)
                    for di, src in enumerate((daf, dtf, csf)):
                        p.op("sync", lambda e, di=di, src=src: e.dma_start(
                            dbg_pre[di], src), s_ynr, 16)
                return nv, st_init

            def emit_scan(prescan_dve, st_init, m1_pe, m1_sy):
                b6 = bank(6)
                cmax = [(s_pe, m1_pe), (s_dve, prescan_dve),
                        (s_gpc, st_init)] + list(m1_sy)
                updd, stbd = {}, {}
                prev_t1d_end = 0
                p.W("sync", s_dve, prescan_dve)
                ncso = p.op("sync", lambda e: e.dma_start(csr_d, csf), s_cso, 16)
                for c in range(SCAN_NCH):
                    for eng in ("pe", "act", "dve", "gp", "sync"):
                        for sm, v in cmax:
                            p.W(eng, sm, v)
                    for t0, t1 in ((0, 8), (8, 16), (16, 24), (24, 32)):
                        p.op("sync", lambda e, c=c, t0=t0, t1=t1: e.dma_start(
                            xall[:, t0 * 256:t1 * 256]
                            .rearrange("p (a t) -> p a t", a=t1 - t0),
                            xbc_d[t0:t1, :, c * 256:(c + 1) * 256]
                            .rearrange("a p t -> p a t")), s_xbc, 16)
                    p.op("sync", lambda e, c=c: e.dma_start(
                        bct, xbc_d[32, :, c * 256:(c + 1) * 256]), s_xbc, 16)
                    dxb = p.op("sync", lambda e, c=c: e.dma_start(
                        cct, xbc_d[33, :, c * 256:(c + 1) * 256]), s_xbc, 16)
                    dzq = [p.op("sync", lambda e, c=c: e.dma_start(
                        zcq[0].rearrange("p (a t) -> p a t", a=8),
                        zs_d[0:8, :, c * 256:(c + 1) * 256]
                        .rearrange("a p t -> p a t")), s_z[0], 16)]
                    cend = c * CH + CH - 1
                    dec = p.op("act", lambda e, c=c, cend=cend: e.activation(
                        decayb, csf[:, c * CH:(c + 1) * CH], AF.Exp, scale=-1.0,
                        bias=csf[:, cend:cend + 1]), s_act, 1)
                    cdn = p.op("act", lambda e, cend=cend: e.activation(
                        cdcol, csf[:, cend:cend + 1], AF.Exp), s_act, 1)
                    p.W("dve", s_act, dec)
                    ddn = p.op("dve", lambda e, c=c: e.tensor_tensor(
                        ddb, dtf[:, c * CH:(c + 1) * CH], decayb, ALU.mult),
                        s_dve, 1)
                    prevev = 0
                    for (srcm, dstm, gate) in ((csf, csT, None), (dtf, dtT, None),
                                               (None, ddT, ddn)):
                        for hh in range(2):
                            if gate:
                                p.W("pe", s_dve, gate)
                            p.W("pe", s_act, prevev)
                            if srcm is not None:
                                n = p.op("pe", lambda e, c=c, hh=hh, srcm=srcm:
                                         e.transpose(
                                    b6[:, hh * 128:hh * 128 + 64],
                                    srcm[:, c * CH + hh * 128:
                                         c * CH + (hh + 1) * 128],
                                    idf[0:64, 0:64]), s_pe, 1)
                            else:
                                n = p.op("pe", lambda e, hh=hh: e.transpose(
                                    b6[:, hh * 128:hh * 128 + 64],
                                    ddb[:, hh * 128:(hh + 1) * 128],
                                    idf[0:64, 0:64]), s_pe, 1)
                            p.W("act", s_pe, n)
                            prevev = p.op("act", lambda e, hh=hh, dstm=dstm: e.copy(
                                dstm[:, hh * 64:(hh + 1) * 64],
                                b6[:, hh * 128:hh * 128 + 64]), s_act, 1)
                    p.W("pe", s_act, cdn)
                    p.W("pe", s_act, prevev)
                    n = p.op("pe", lambda e: e.transpose(b6[0:1, 256:320], cdcol,
                                                         idf[0:64, 0:64]), s_pe, 1)
                    p.W("act", s_pe, n)
                    ncd = p.op("act", lambda e: e.copy(cdrow, b6[0:1, 256:320]),
                               s_act, 1)
                    p.W("pe", s_act, ncd)
                    n = p.op("pe", lambda e: e.matmul(b6[:, 384:448], onesrow[:],
                                                      cdrow, start=True,
                                                      stop=True), s_pe, 1)
                    p.W("dve", s_pe, n)
                    ncdr = p.op("dve", lambda e: e.tensor_copy(
                        cdecrep, b6[:, 384:448]), s_dve, 1)

                    p.W("pe", s_act, dec)
                    p.W("pe", s_dve, ncdr)
                    for hh in range(2):
                        p.W("pe", s_act, prevev)
                        n = p.op("pe", lambda e, hh=hh: e.transpose(
                            b6[:, hh * 128:hh * 128 + 64],
                            decayb[:, hh * 128:(hh + 1) * 128],
                            idf[0:64, 0:64]), s_pe, 1)
                        p.W("act", s_pe, n)
                        prevev = p.op("act", lambda e, hh=hh: e.copy(
                            decT[:, hh * 64:(hh + 1) * 64],
                            b6[:, hh * 128:hh * 128 + 64]), s_act, 1)

                    p.W("pe", s_xbc, dxb)
                    for stn in range(2):
                        p.op("pe", lambda e, stn=stn: e.transpose(
                            PSB[:, stn * 128:(stn + 1) * 128],
                            bct[:, stn * 128:(stn + 1) * 128], idb), s_pe, 1)
                    ntb2 = s_pe.n
                    ntl = []
                    for stn in range(2):
                        p.W("act", s_pe, ntb2)
                        ntl.append(p.op("act", lambda e, stn=stn: e.copy(
                            btok[:, stn * 128:(stn + 1) * 128],
                            PSB[:, stn * 128:(stn + 1) * 128]), s_act, 1))
                    nm1 = []
                    for stn in range(2):
                        if stn == 1:
                            p.W("pe", s_dve, nm1[0])
                        n = p.op("pe", lambda e, stn=stn: e.matmul(
                            bank(5)[:, stn * 256:(stn + 1) * 256],
                            bct[:, stn * 128:(stn + 1) * 128], cct,
                            start=True, stop=True), s_pe, 1)
                        p.W("dve", s_pe, n)
                        p.W("dve", s_c, nconst)
                        nm1.append(p.op("dve", lambda e, stn=stn: e.tensor_tensor(
                            m1t[:, stn * 256:(stn + 1) * 256],
                            bank(5)[:, stn * 256:(stn + 1) * 256],
                            maskc[:, stn * 256:(stn + 1) * 256],
                            ALU.mult), s_dve, 1))
                    d1eb, d1lt, d1le, d1mm = {}, {}, {}, {}
                    nga, ncet = {}, {}
                    xdone, xtr, t1d, pe_y, xddn = {}, {}, {}, {}, {}
                    csid = {}

                    d1ev, s4d, nt1x = {}, {}, {}

                    def emit_post(j):
                        ybj = bank(2 + (j % 2), 256)
                        qj = j // 8
                        if SCAN_SUB >= 5:
                            p.W("gp", s_pe, s4d[j])
                            p.W("gp", s_dve, ncdr)
                            p.W("gp", s_gpc, nga.get((j, 1), 0))
                            p.op("gp", lambda e, j=j: e.tensor_tensor(
                                t2x.rearrange("p (a b) -> p a b", a=2),
                                stf[:, j * 128:(j + 1) * 128]
                                .rearrange("p (a b) -> p a b", a=2),
                                cdecrep[:, 2 * j:2 * j + 2].unsqueeze(2)
                                .broadcast_to([128, 2, 64]), ALU.mult), s_gpc, 1)
                            p.W("dve", s_gpc, s_gpc.n)
                            nup = p.op("dve", lambda e, j=j: e.tensor_tensor(
                                stf[:, j * 128:(j + 1) * 128], t2x,
                                bank(4)[:, (j % 2) * 128:(j % 2) * 128 + 128],
                                ALU.add), s_dve, 1)
                            updd[(c, j)] = nup
                            p.W("gp", s_dve, nup)
                            stbd[(c, j)] = p.op("gp", lambda e, j=j:
                                                e.tensor_copy(
                                stb[:, j * 128:(j + 1) * 128],
                                stf[:, j * 128:(j + 1) * 128]), s_gpc, 1)
                        if SCAN_SUB >= 6:
                            p.W("dve", s_pe, pe_y[j])
                            p.W("dve", s_c, nconst)
                            p.W("dve", s_xbc, dxb)
                            nt1x[j] = p.op("dve", lambda e, j=j, ybj=ybj:
                                           e.scalar_tensor_tensor(
                                t1x, xall[:, j * 256:(j + 1) * 256],
                                dpair[:, j:j + 1], ybj,
                                ALU.mult, ALU.add), s_dve, 1)
                            p.W("gp", s_dve, nt1x[j])
                            p.W("gp", s_z[qj % 2], dzq[qj])
                            t1d[j] = p.op("gp", lambda e, j=j, qj=qj:
                                          e.tensor_tensor(
                                ygb[:, j * 256:(j + 1) * 256], t1x,
                                zcq[qj % 2][:, (j % 8) * 256:(j % 8 + 1) * 256],
                                ALU.mult), s_gpc, 1)
                        else:
                            nt1x[j] = s_dve.n

                    def emit_xt(k):
                        p.W("pe", s_xbc, dxb)
                        p.W("pe", s_act, ntl[-1])
                        p.W("pe", s_act, xdone.get(k - 1, 0))
                        p.W("pe", s_dve, xddn.get(k - 1, 0))
                        for stn in range(2):
                            p.op("pe", lambda e, k=k, stn=stn: e.transpose(
                                PSB[:, 512 + ((k % 2) * 2 + stn) * 128:
                                    512 + ((k % 2) * 2 + stn) * 128 + 128],
                                xall[:, k * 256 + stn * 128:
                                     k * 256 + (stn + 1) * 128], idb),
                                s_pe, 1)
                        xtr[k] = s_pe.n
                        for stn in range(2):
                            p.W("act", s_pe, xtr[k])
                            for half in range(2):
                                base = 512 + ((k % 2) * 2 + stn) * 128 + half * 64
                                p.op("act", lambda e, k=k, stn=stn, half=half,
                                     base=base: e.activation(
                                    xdt[stn][:, k * 128 + half * 64:
                                             k * 128 + half * 64 + 64],
                                    PSB[:, base:base + 64], AF.Copy,
                                    scale=dtT[:, stn * 64 + 2 * k + half:
                                              stn * 64 + 2 * k + half + 1]),
                                    s_act, 1)
                        xdone[k] = s_act.n
                        for stn in range(2):
                            p.W("dve", s_act, xdone[k])
                            for half in range(2):
                                p.op("dve", lambda e, k=k, stn=stn, half=half:
                                     e.tensor_scalar(
                                    xdd[stn][:, k * 128 + half * 64:
                                             k * 128 + half * 64 + 64],
                                    xdt[stn][:, k * 128 + half * 64:
                                             k * 128 + half * 64 + 64],
                                    decT[:, stn * 64 + 2 * k + half:
                                         stn * 64 + 2 * k + half + 1],
                                    None, ALU.mult), s_dve, 1)
                        xddn[k] = s_dve.n

                    def csr_in(j, gate_pe):
                        p.W("sync", s_pe, gate_pe)
                        p.W("sync", s_cso, ncso)
                        csid[j] = p.op("sync", lambda e, j=j, c=c: e.dma_start(
                            csrg[j % 4].rearrange("o (h t) -> o h t", h=4),
                            csr_d[4 * j:4 * j + 4, c * CH:(c + 1) * CH]
                            .unsqueeze(0)), s_csi[j % 4], 16)

                    def emit_bcast(k):
                        p.W("pe", s_csi[(k // 2) % 4], csid[k // 2])
                        p.W("pe", s_dve, d1ev.get(k - 2, 0))
                        d1mm[k] = p.op("pe", lambda e, k=k: e.matmul(
                            bank(k % 2), onesrow[:],
                            csrg[(k // 2) % 4][0:1,
                                               (k % 2) * 512:(k % 2) * 512 + 512],
                            start=True, stop=True), s_pe, 1)
                        if k % 2 == 1 and k // 2 + 4 < 16:
                            csr_in(k // 2 + 4, d1mm[k])

                    for j in range(4):
                        csr_in(j, 0)
                    for i in range(SCAN_NI):
                        q = i // 8
                        if SCAN_SUB >= 6 and i % 8 == 1 and q + 1 < 4:
                            if q >= 1:
                                p.W("sync", s_gpc, t1d.get(8 * q - 1, 0))
                            dzq.append(p.op("sync", lambda e, c=c, q=q: e.dma_start(
                                zcq[(q + 1) % 2].rearrange("p (a t) -> p a t", a=8),
                                zs_d[8 * (q + 1):8 * (q + 2), :,
                                     c * 256:(c + 1) * 256]
                                .rearrange("a p t -> p a t")), s_z[(q + 1) % 2], 16))
                        dbk = i % 2
                        if i == 0:
                            emit_bcast(0)
                            emit_bcast(1)
                        nd1 = d1mm[i]
                        p.W("dve", s_pe, nd1)
                        p.W("dve", s_act, d1eb.get(i - 2, 0))
                        nevac = p.op("dve", lambda e, dbk=dbk, i=i: e.tensor_copy(
                            csb2[i % 2], bank(dbk)), s_dve, 1)
                        d1ev[i] = nevac
                        p.W("act", s_dve, nevac)
                        p.W("act", s_gpc, ncet.get(i - 2, 0))
                        neb = p.op("act", lambda e, i=i: e.activation(
                            ebx2[i % 2], csb2[i % 2], AF.Exp), s_act, 1)
                        d1eb[i] = neb
                        if i + 2 < SCAN_NI:
                            emit_bcast(i + 2)
                        if SCAN_SUB >= 2:
                            for hh_i in range(2):
                                hh = 2 * i + hh_i
                                p.W("dve", s_act, d1le.get((i - 2, hh_i), 0))
                                for stn in range(2):
                                    p.op("dve", lambda e, stn=stn,
                                         hh=hh, hh_i=hh_i, i=i: e.tensor_scalar(
                                        ltm4[i % 2][hh_i][:, stn * 256:(stn + 1) * 256],
                                        csb2[i % 2][:, hh_i * 256:(hh_i + 1) * 256],
                                        csT[:, stn * 64 + hh:stn * 64 + hh + 1],
                                        0.0, ALU.subtract, ALU.min), s_dve, 1)
                                d1lt[(i, hh_i)] = s_dve.n
                                p.W("act", s_dve, d1lt[(i, hh_i)])
                                p.W("act", s_gpc, nga.get((i - 2, hh_i), 0))
                                d1le[(i, hh_i)] = p.op(
                                    "act", lambda e, hh_i=hh_i, i=i: e.activation(
                                        ltx4[i % 2][hh_i], ltm4[i % 2][hh_i],
                                        AF.Exp), s_act, 1)
                                p.W("gp", s_act, neb)
                                p.W("gp", s_xbc, dxb)
                                p.W("gp", s_pe, pe_y.get(i - 2, 0))
                                p.op("gp", lambda e, hh_i=hh_i, i=i:
                                     e.tensor_tensor(
                                    cet4[i % 2][hh_i], cct,
                                    ebx2[i % 2][:, hh_i * 256:(hh_i + 1) * 256],
                                    ALU.mult), s_gpc, 1)
                                ncet[i] = s_gpc.n
                                p.W("gp", s_act, d1le[(i, hh_i)])
                                p.W("gp", s_dve, nm1[-1])
                                p.op("gp", lambda e, hh_i=hh_i, i=i:
                                     e.tensor_tensor(
                                    a1t4[i % 2][hh_i],
                                    ltx4[i % 2][hh_i], m1t, ALU.mult), s_gpc, 1)
                                nga[(i, hh_i)] = s_gpc.n
                        if SCAN_SUB < 3:
                            continue
                        if i == 0:
                            emit_xt(0)
                        if i + 1 < SCAN_NI:
                            emit_xt(i + 1)
                        if SCAN_SUB < 4:
                            continue
                        if i >= 1:
                            emit_post(i - 1)
                        yb = bank(2 + (i % 2), 256)
                        p.W("pe", s_act, xdone[i])
                        p.W("pe", s_dve, xddn[i])
                        p.W("pe", s_gpc, nga.get((i, 1), 0))
                        p.W("pe", s_gpc, ncet.get(i, 0))
                        p.W("pe", s_dve, nt1x.get(i - 2, 0))
                        if c > 0:
                            p.W("pe", s_gpc, stbd.get((c - 1, i), 0))
                        for half, asl in ((0, 0), (1, 1)):
                            r0 = half * 64
                            for stn in range(2):
                                p.op("pe", lambda e, i=i, stn=stn, half=half,
                                     asl=asl, yb=yb, r0=r0: e.matmul(
                                    yb[r0:r0 + 64, :],
                                    xdt[stn][:, i * 128 + half * 64:
                                             i * 128 + half * 64 + 64],
                                    a1t4[i % 2][asl][:, stn * 256:(stn + 1) * 256],
                                    start=(stn == 0), stop=False), s_pe, 1)
                            p.op("pe", lambda e, i=i, half=half, asl=asl, yb=yb,
                                 r0=r0: e.matmul(
                                yb[r0:r0 + 64, :],
                                stb[:, i * 128 + half * 64:
                                    i * 128 + half * 64 + 64],
                                cet4[i % 2][asl], start=False, stop=True), s_pe, 1)
                        npy = s_pe.n
                        pe_y[i] = npy
                        if SCAN_SUB < 5:
                            continue
                        p.W("pe", s_dve, updd.get((c, i - 1) if i >= 1
                                                  else (c - 1, 31), 0))
                        p.W("pe", s_dve, nm1[-1])
                        for stn in range(2):
                            p.op("pe", lambda e, i=i, stn=stn: e.matmul(
                                bank(4)[:, (i % 2) * 128:(i % 2) * 128 + 128],
                                btok[:, stn * 128:(stn + 1) * 128],
                                xdd[stn][:, i * 128:(i + 1) * 128],
                                start=(stn == 0), stop=(stn == 1)), s_pe, 1)
                        s4d[i] = s_pe.n
                    if SCAN_SUB >= 4 and SCAN_NI >= 1:
                        emit_post(SCAN_NI - 1)
                    prev_t1d_end = t1d.get(SCAN_NI - 1, 0)
                    if SCAN_NI < 32:
                        cmax = [(s_pe, s_pe.n), (s_act, s_act.n),
                                (s_dve, s_dve.n), (s_gpc, s_gpc.n),
                                (s_xbc, s_xbc.n), (s_z[0], s_z[0].n),
                                (s_z[1], s_z[1].n), (s_csi[0], s_csi[0].n),
                                (s_csi[1], s_csi[1].n), (s_ynw, s_ynw.n)]
                        prev_t1d_end = t1d.get(SCAN_NI - 1, 0)
                        continue
                    npe_s = 0
                    nsq = {}
                    for kk in range(32):
                        p.W("act", s_gpc, t1d[kk])
                        p.W("act", s_pe, npe_s)
                        nsq[kk] = p.op("act", lambda e, kk=kk: e.activation(
                            sqx[kk % 2], ygb[:, kk * 256:(kk + 1) * 256],
                            AF.Square), s_act, 1)
                        p.W("pe", s_act, nsq[kk])
                        if kk == 0:
                            p.W("pe", s_act, prevev)
                        npe_s = p.op("pe", lambda e, kk=kk: e.matmul(
                            b6[0:1, 0:256], onescol[:], sqx[kk % 2],
                            start=(kk == 0), stop=(kk == 31)), s_pe, 1)
                    p.W("act", s_pe, npe_s)
                    p.op("act", lambda e: e.activation(
                        srms[:, 0:256], b6[0:1, 0:256], AF.Ln, scale=1.0 / DI,
                        bias=epsc), s_act, 1)
                    nrs = p.op("act", lambda e: e.activation(
                        srms[:, 0:256], srms[:, 0:256], AF.Exp, scale=-0.5),
                        s_act, 1)
                    p.W("pe", s_act, nrs)
                    p.W("pe", s_dve, nm1[-1])
                    n = p.op("pe", lambda e: e.matmul(
                        bank(5)[:, 0:256], onesrow[:], srms[:, 0:256],
                        start=True, stop=True), s_pe, 1)
                    p.W("dve", s_pe, n)
                    nrb = p.op("dve", lambda e: e.tensor_copy(
                        rbx, bank(5)[:, 0:256]), s_dve, 1)
                    nyn = {}
                    for kk in range(32):
                        p.W("dve", s_pe, npe_s)
                        p.W("dve", s_dve, nrb)
                        nyn[kk] = p.op("dve", lambda e, kk=kk:
                                       e.scalar_tensor_tensor(
                            ygb[:, kk * 256:(kk + 1) * 256],
                            ygb[:, kk * 256:(kk + 1) * 256],
                            normw[:, kk:kk + 1], rbx, ALU.mult, ALU.mult),
                            s_dve, 1)
                    for g4 in range(4):
                        p.W("sync", s_dve, nyn[g4 * 8 + 7])
                        p.op("sync", lambda e, c=c, g4=g4: e.dma_start(
                            yn_d[g4 * 8:(g4 + 1) * 8, :, c * 256:(c + 1) * 256]
                            .rearrange("a p t -> p a t"),
                            ygb[:, g4 * 8 * 256:(g4 + 1) * 8 * 256]
                            .rearrange("p (a t) -> p a t", a=8)), s_ynw, 16)
                    cmax = [(s_pe, s_pe.n), (s_act, s_act.n), (s_dve, s_dve.n),
                            (s_gpc, s_gpc.n), (s_xbc, s_xbc.n),
                            (s_z[0], s_z[0].n), (s_z[1], s_z[1].n),
                            (s_csi[0], s_csi[0].n), (s_csi[1], s_csi[1].n),
                            (s_ynw, s_ynw.n)]
                return cmax

            def emit_m2(scan_end, ffn1_xo):
                for sm, v in ffn1_xo:
                    p.W("sync", sm, v)
                npf = p.op("sync", lambda e: e.dma_start(
                    x2_d, x1_d), s_pfs[1], 16)
                for eng in ("pe", "act", "dve", "gp", "sync"):
                    for sm, v in scan_end:
                        p.W(eng, sm, v)
                ynl = []
                for kk in range(32):
                    ynl.append(p.op("sync", lambda e, kk=kk: e.dma_start(
                        BIG[:, kk * 2048:(kk + 1) * 2048], yn_d[kk]), s_ynr, 16))
                wdma, ped, dvd, xod = [], [], [], []

                def wout_load(dmt):
                    if dmt >= 2:
                        p.W("gp", s_pe, ped[dmt - 2])
                    wdma.append(p.op("gp", lambda e, dmt=dmt: e.dma_start(
                        woutb[dmt % 2],
                        wout_d[dmt].rearrange("p k h -> p (k h)")),
                        s_wo[dmt % 2], 16))

                wout_load(0)
                wout_load(1)
                for dmt in range(16):
                    p.W("pe", s_wo[dmt % 2], wdma[dmt])
                    p.W("pe", s_ynr, ynl[-1])
                    if dmt >= 2:
                        p.W("pe", s_dve, dvd[dmt - 2])
                    for k in range(32):
                        for tb in range(NTB):
                            p.op("pe", lambda e, dmt=dmt, k=k, tb=tb: e.matmul(
                                bank((dmt % 2) * 4 + tb),
                                woutb[dmt % 2][:, k * 128:(k + 1) * 128],
                                BIG[:, k * 2048 + tb * TB:
                                    k * 2048 + (tb + 1) * TB],
                                start=(k == 0), stop=(k == 31)),
                                s_pe if (k == 31 and tb == NTB - 1) else None)
                    ped.append(s_pe.n)
                    if dmt + 2 < 16:
                        wout_load(dmt + 2)
                    nlast = 0
                    for tb in range(NTB):
                        j = dmt * 4 + tb
                        p.W("dve", s_pe, ped[dmt])
                        if len(xod) >= 2:
                            p.W("dve", s_xg[j % 2], xod[-2])
                        nlast = p.op("dve", lambda e, dmt=dmt, tb=tb, j=j:
                                     e.tensor_copy(
                            xoutb[j % 2], bank((dmt % 2) * 4 + tb)), s_dve, 1)
                        p.W("gp", s_dve, nlast)
                        p.W("gp", s_pfs[1], npf)
                        xod.append(p.op("gp", lambda e, dmt=dmt, tb=tb, j=j:
                                        e.dma_start(
                            x2_d[dmt * 128:(dmt + 1) * 128,
                                 tb * TB:(tb + 1) * TB], xoutb[j % 2],
                            accum_op=ALU.add), s_xg[j % 2], 16))
                    dvd.append(nlast)
                return ped[-1], dvd[-1], [(s_xg[0], s_xg[0].n),
                                          (s_xg[1], s_xg[1].n)]

            def finish_early():
                nf = p.op("sync", lambda e: e.dma_start(
                    xo_d[0:128, :], lnx[0]), s_xou[0], 16)
                p.W("act", s_xou[0], nf)
                p.W("sync", s_xou[0], nf)

            ln1 = emit_ln(u_d, [(s_pe, p1_pe_end),
                                (s_dve, p1_dve_end)] + w2f_ready)
            if STOP <= 2:
                return finish_early()
            ffn1_pe, ffn1_ev, ffn1_xo = emit_ffn(u_d, x1_d, ln1,
                                                s_pfs[1], [])
            if STOP <= 3:
                return finish_early()
            ln2 = emit_ln(x1_d, [(s_pe, ffn1_pe),
                                 (s_dve, ffn1_ev)] + ffn1_xo)
            if STOP <= 4:
                return finish_early()
            m1_pe, m1_dve, m1_sy = emit_m1(ln2)
            if STOP <= 5:
                return finish_early()
            psc, st_init = emit_prescan(m1_dve)
            if STOP <= 6:
                return finish_early()
            scan_end = emit_scan(psc, st_init, m1_pe, m1_sy)
            if STOP <= 7:
                return finish_early()
            m2_pe, m2_dve, m2_xo = emit_m2(scan_end, ffn1_xo)
            if STOP <= 8:
                return finish_early()
            ln3 = emit_ln(x2_d, [(s_pe, m2_pe), (s_dve, m2_dve)] + m2_xo)
            if STOP <= 9:
                return finish_early()
            ffn3_pe, ffn3_ev, ffn3_xo = emit_ffn(x2_d, xo_d, ln3,
                                                s_pfs[3], m2_xo)
            for sm, v in ffn3_xo:
                p.W("act", sm, v)
                p.W("sync", sm, v)

        block = st.enter_context(nc.Block())

        def run_pass(name):
            def f(eng):
                sems = {k: Sem(v, k) for k, v in sem_handles.items()}
                emit(P(name, eng, sems))
            return f

        block.tensor(run_pass("pe"))
        block.scalar(run_pass("act"))
        block.vector(run_pass("dve"))
        block.gpsimd(run_pass("gp"))
        block.sync(run_pass("sync"))

    return nc


_NC_CACHE = {}


def kernel(**inputs):
    u = np.asarray(inputs["L0_attn_u"], dtype=np.float32)
    wi = np.asarray(inputs["L0_attn_in_proj_w"], dtype=np.float32)
    cw = np.asarray(inputs["L0_attn_conv_w"], dtype=np.float32)
    cb = np.asarray(inputs["L0_attn_conv_b"], dtype=np.float32)
    dtb = np.asarray(inputs["L0_attn_dt_bias"], dtype=np.float32)
    alog = np.asarray(inputs["L0_attn_A_log"], dtype=np.float32)
    Dv = np.asarray(inputs["L0_attn_D"], dtype=np.float32)
    nw = np.asarray(inputs["L0_attn_norm_weight"], dtype=np.float32)
    wo = np.asarray(inputs["L0_attn_out_proj_w"], dtype=np.float32)
    W1 = np.asarray(inputs["L0_ffn_W1"], dtype=np.float32)
    W2 = np.asarray(inputs["L0_ffn_W2"], dtype=np.float32)
    W3 = np.asarray(inputs["L0_ffn_W3"], dtype=np.float32)

    w1a = np.ascontiguousarray(W1.T.reshape(NK, 128, NA, 128).transpose(2, 1, 0, 3))
    w3a = np.ascontiguousarray(W3.T.reshape(NK, 128, NA, 128).transpose(2, 1, 0, 3))
    w2q = np.ascontiguousarray(
        W2.T.reshape(64, 128, 2, 1024).transpose(0, 2, 1, 3).reshape(128, 64 * 2048))
    wip = np.zeros((FIN * 128, D), np.float32)
    wip[:8512] = wi
    wina = np.ascontiguousarray(
        wip.T.reshape(NK, 128, FIN, 128).transpose(2, 1, 0, 3))
    wouta = np.ascontiguousarray(
        wo.T.reshape(32, 128, 16, 128).transpose(2, 1, 0, 3))
    convw = np.ascontiguousarray(cw.reshape(34, 128, 4).transpose(1, 0, 2)
                                 .reshape(128, 136))
    convb = np.ascontiguousarray(cb.reshape(34, 128).T)
    dpair = np.ascontiguousarray(
        Dv.reshape(32, 2).T.reshape(2, 1, 32).repeat(64, axis=1).reshape(128, 32))
    normw = np.ascontiguousarray(nw.reshape(32, 128).T)

    shared = dict(w1a=w1a, w3a=w3a, w2q=w2q, wina=wina, wouta=wouta,
                  convw_d=convw, convb_d=convb, dtbias_d=dtb.reshape(64, 1),
                  alog_d=alog.reshape(64, 1), dpair_d=dpair, normw_d=normw)
    in_maps = []
    for c in range(8):
        m = dict(shared)
        m["u"] = np.ascontiguousarray(u[c].T)
        in_maps.append(m)

    nc = _NC_CACHE.get("nc")
    if nc is None:
        nc = build_program()
        _NC_CACHE["nc"] = nc
    res = run_bass_kernel_spmd(nc, in_maps, core_ids=list(range(8)))
    out = np.stack([np.ascontiguousarray(res.results[c]["xo"].T)
                    for c in range(8)], axis=0)
    return out
